# version 16
# baseline (speedup 1.0000x reference)
import numpy as np
import ml_dtypes
import concourse.bass as bass
import concourse.mybir as mybir
from contextlib import ExitStack

F32 = mybir.dt.float32
BF16 = mybir.dt.bfloat16
AF = mybir.ActivationFunctionType
ALU = mybir.AluOpType
AX = mybir.AxisListType

D = 1024
DI = 2048
DS = 16
NE = 8
EPS = 1e-5


class Sem:
    def __init__(self, es, nc, name):
        self.h = es.enter_context(nc.semaphore(name))
        self.n = 0
        self.name = name


class Prog:
    ENGS = ("sync", "scalar", "vector", "tensor", "gpsimd")

    def __init__(self):
        self.p = {k: [] for k in self.ENGS}
        self.last_wait = {}

    def em(self, eng, fn, inc=None, k=1):
        v = None
        if inc is not None:
            inc.n += k
            v = inc.n

        def th(e, fn=fn, inc=inc, k=k):
            ins = fn(e)
            if inc is not None:
                ins.then_inc(inc.h, k)
        self.p[eng].append(th)
        return v

    def wt(self, eng, sem, v):
        if v is None or v <= 0:
            return
        key = (eng, sem.name)
        if self.last_wait.get(key, 0) >= v:
            return
        self.last_wait[key] = v
        self.p[eng].append(lambda e, s=sem, v=v: e.wait_ge(s.h, v))


class _Trunc(Exception):
    pass


def build(T=4096, TC=512, dump=False, trunc=None, dumpc=0, sync_stages=(),
          sched=None):
    nch = T // TC
    nti = TC // 128
    nc = bass.Bass("TRN2", target_bir_lowering=False)

    x_d = nc.dram_tensor("x", [T, D], F32, kind="ExternalInput").ap()
    nw_d = nc.dram_tensor("norm_w", [1, D], F32, kind="ExternalInput").ap()
    nb_d = nc.dram_tensor("norm_b", [1, D], F32, kind="ExternalInput").ap()
    win_d = nc.dram_tensor("W_in", [D, DI], F32, kind="ExternalInput").ap()
    wgate_d = nc.dram_tensor("W_gate", [D, DI], F32, kind="ExternalInput").ap()
    wdt_d = nc.dram_tensor("W_dt", [DI, DI], F32, kind="ExternalInput").ap()
    bdt_d = nc.dram_tensor("b_dt", [DI // 128, 128], F32, kind="ExternalInput").ap()
    wb_d = nc.dram_tensor("W_B", [DI, DS], F32, kind="ExternalInput").ap()
    wc_d = nc.dram_tensor("W_C", [DI, DS], F32, kind="ExternalInput").ap()
    alog_d = nc.dram_tensor("A_log", [DI, DS], F32, kind="ExternalInput").ap()
    wout_d = nc.dram_tensor("W_out", [DI, D], F32, kind="ExternalInput").ap()
    rw_d = nc.dram_tensor("router_w", [D, NE], F32, kind="ExternalInput").ap()
    w1_d = nc.dram_tensor("w1", [NE * D, D], F32, kind="ExternalInput").ap()
    w2_d = nc.dram_tensor("w2", [NE * D, D], F32, kind="ExternalInput").ap()
    sgu_d = nc.dram_tensor("sgu", [D, 2 * D], F32, kind="ExternalInput").ap()
    sdn_d = nc.dram_tensor("sdown", [D, D], F32, kind="ExternalInput").ap()
    sgw_d = nc.dram_tensor("sgw", [D // 128, 128], F32, kind="ExternalInput").ap()
    out_d = nc.dram_tensor("out", [T, D], F32, kind="ExternalOutput").ap()

    ws_in = nc.dram_tensor("ws_in", [128, 2 * 8192], BF16, kind="Internal").ap()
    ws_gate = nc.dram_tensor("ws_gate", [128, 2 * 8192], BF16, kind="Internal").ap()
    ws_dt = nc.dram_tensor("ws_dt", [128, 4 * 8192], BF16, kind="Internal").ap()
    ws_out = nc.dram_tensor("ws_out", [128, 2 * 8192], BF16, kind="Internal").ap()
    ws_w1 = nc.dram_tensor("ws_w1", [128, 8 * 8192], BF16, kind="Internal").ap()
    ws_w2 = nc.dram_tensor("ws_w2", [128, 8 * 8192], BF16, kind="Internal").ap()
    ws_sgu = nc.dram_tensor("ws_sgu", [128, 2 * 8192], BF16, kind="Internal").ap()
    ws_sdn = nc.dram_tensor("ws_sdn", [128, 8192], BF16, kind="Internal").ap()

    ident_bf_d = nc.inline_tensor(np.eye(128, dtype=ml_dtypes.bfloat16), "identbf").ap()
    oh16_np = np.zeros((16, 16 * 128), dtype=ml_dtypes.bfloat16)
    for s_ in range(16):
        oh16_np[s_, s_ * 128:(s_ + 1) * 128] = 1
    oh16_d = nc.inline_tensor(oh16_np, "oh16c").ap()
    ones1_d = nc.inline_tensor(np.ones((1, 128), dtype=np.float32), "ones1c").ap()

    dumps = {}
    if dump:
        for nm, shape, dt in [
            ("d_z", [128, 16 * TC], BF16), ("d_gate", [128, 16 * TC], BF16),
            ("d_dt", [128, 16 * TC], BF16), ("d_Bt", [16, TC], BF16),
            ("d_Ct", [16, TC], BF16), ("d_yg", [128, 16 * TC], BF16),
            ("d_mo", [128, 8 * TC], BF16), ("d_xf", [128, 8 * TC], BF16),
            ("d_comb", [128, (TC // 128) * 8], F32), ("d_xn", [128, 8 * TC], BF16),
            ("d_shx", [128, 8 * TC], BF16), ("d_ffn", [128, 8 * TC], BF16),
        ]:
            dumps[nm] = nc.dram_tensor(nm, shape, dt, kind="ExternalOutput").ap()

    P = Prog()
    es = ExitStack()
    S = {k: Sem(es, nc, k) for k in (
        "STG", "LD", "W", "X", "X2", "ST", "PE", "ACT", "DVE",
        "SCAN", "POOL", "RED", "YG")}
    STG, LD, W, X, X2, ST, PE, ACT, DVE, SCAN, POOL, RED, YG = (
        S[k] for k in ("STG", "LD", "W", "X", "X2", "ST", "PE", "ACT", "DVE",
                       "SCAN", "POOL", "RED", "YG"))

    def sb(name, shape, dt, stack=es):
        return stack.enter_context(nc.sbuf_tensor(name, shape, dt))

    def psum(name, shape, dt, stack):
        return stack.enter_context(nc.psum_tensor(name, shape, dt))

    _ck = [0]

    def ckpt(nm):
        _ck[0] += 1
        if trunc is not None and _ck[0] >= trunc:
            print(f"TRUNC at ckpt {_ck[0]}: {nm}", flush=True)
            raise _Trunc
        if nm in sync_stages or "all" in sync_stages:
            for eng in ("sync", "scalar", "vector", "tensor", "gpsimd"):
                for sem_ in (DVE, ACT, PE, SCAN, RED, POOL, YG, X2, ST, W, X,
                             STG, LD):
                    P.wt(eng, sem_, sem_.n)

    ident_bf = sb("ident_bf", [128, 128], BF16)
    oh16 = sb("oh16", [16, 16 * 128], BF16)
    ones1b = oh16[0:1, 0:128]
    oh8 = oh16[0:8, 0:8 * 128]
    nw_rep = sb("nw_rep", [128, D], BF16)
    nb_rep = sb("nb_rep", [128, D], BF16)
    aneg = sb("aneg", [128, 16 * DS], F32)
    bdt_sb = sb("bdt_sb", [128, 16], F32)
    wbb = sb("wbb", [128, 16 * DS], BF16)
    wcb = sb("wcb", [128, 16 * DS], BF16)
    rwb = sb("rwb", [128, 8 * NE], BF16)
    sgwb = sb("sgwb", [128, 8], BF16)
    hcol = sb("hcol", [128, 256], BF16)
    cvals = sb("cvals", [128, 8], F32)
    es_p0x = ExitStack()
    ones1 = sb("ones1", [1, 128], F32, es_p0x)
    nrow = sb("nrow", [1, D], F32, es_p0x)
    atmp = sb("atmp", [128, 16 * DS], F32, es_p0x)

    def dump_barrier():
        for _eng in ("scalar", "vector", "tensor", "gpsimd"):
            P.wt(_eng, ST, ST.n)

    for _i, _v in enumerate([EPS, 1.0 / D, -0.5, -1.0, 1.0, 0.001]):
        nc.const_aps.register(F32, _v, cvals[:, _i:_i + 1])
        P.em("vector", lambda e, _i=_i, _v=_v: e.memset(cvals[:, _i:_i + 1], _v))
    stage_ms = {}
    for nm, dst, srcd, nsub, ncols in [
        ("in", ws_in, win_d, 2, DI), ("gate", ws_gate, wgate_d, 2, DI),
        ("dt", ws_dt, wdt_d, 4, DI), ("out", ws_out, wout_d, 2, D),
        ("w1", ws_w1, w1_d, 8, D), ("w2", ws_w2, w2_d, 8, D),
        ("sgu", ws_sgu, sgu_d, 2, DI), ("sdn", ws_sdn, sdn_d, 1, D),
    ]:
        nk = 8192 // ncols
        for si in range(nsub):
            stage_ms[nm] = P.em("gpsimd", lambda e, d=dst, s=srcd, si=si, nk=nk,
                                ncols=ncols: e.dma_start(
                out=d[:, si * 8192:(si + 1) * 8192].rearrange(
                    "p (kb n) -> p kb n", n=ncols),
                in_=s[si * nk * 128:(si + 1) * nk * 128, :].rearrange(
                    "(kb p) n -> p kb n", p=128)), inc=STG, k=16)
    wb_ld = P.em("gpsimd", lambda e: e.dma_start(
        out=wbb[:].rearrange("p (kb s) -> p kb s", s=DS),
        in_=wb_d.rearrange("(kb p) s -> p kb s", p=128)), inc=STG, k=16)
    wc_ld = P.em("gpsimd", lambda e: e.dma_start(
        out=wcb[:].rearrange("p (kb s) -> p kb s", s=DS),
        in_=wc_d.rearrange("(kb p) s -> p kb s", p=128)), inc=STG, k=16)
    rw_ld = P.em("gpsimd", lambda e: e.dma_start(
        out=rwb[:].rearrange("p (kb s) -> p kb s", s=NE),
        in_=rw_d.rearrange("(kb p) s -> p kb s", p=128)), inc=STG, k=16)
    sgw_ld = None
    for kb in range(8):
        sgw_ld = P.em("gpsimd", lambda e, kb=kb: e.dma_start(
            out=sgwb[:, kb:kb + 1],
            in_=sgw_d.rearrange("a p -> p a")[:, kb:kb + 1]), inc=STG, k=16)
    for dst, src in [(ident_bf, ident_bf_d),
                     (oh16, oh16_d), (ones1, ones1_d)]:
        P.em("sync", lambda e, d=dst, s=src: e.dma_start(out=d[:], in_=s[:, :]),
             inc=LD, k=16)
    nwld = P.em("sync", lambda e: e.dma_start(out=nrow[:, :], in_=nw_d[:, :]), inc=LD, k=16)
    P.em("sync", lambda e: e.dma_start(
        out=atmp[:].rearrange("p (db s) -> p db s", s=DS),
        in_=alog_d.rearrange("(db p) s -> p db s", p=128)), inc=LD, k=16)
    for kb in range(16):
        P.em("sync", lambda e, kb=kb: e.dma_start(
            out=bdt_sb[:, kb:kb + 1],
            in_=bdt_d.rearrange("a p -> p a")[:, kb:kb + 1]), inc=LD, k=16)
    consts_ld = LD.n

    es_p0 = ExitStack()
    ps0_t = psum("ps0t", [128, 512], F32, es_p0)
    P.wt("tensor", LD, nwld)
    nrep_last = 0
    for pass_, dst in [(0, nw_rep), (1, nb_rep)]:
        if pass_ == 1:
            P.wt("sync", ACT, nrep_last)
            nbld = P.em("sync", lambda e: e.dma_start(out=nrow[:, :], in_=nb_d[:, :]),
                        inc=LD, k=16)
            P.wt("tensor", LD, nbld)
        for j in range(D // 512):
            m = P.em("tensor", lambda e, j=j: e.matmul(
                ps0_t[:, :], ones1[:, :], nrow[:, j * 512:(j + 1) * 512],
                start=True, stop=True), inc=PE)
            P.wt("scalar", PE, m)
            ev = P.em("scalar", lambda e, d=dst, j=j: e.activation(
                out=d[:, j * 512:(j + 1) * 512], in_=ps0_t[:, :], func=AF.Copy), inc=ACT)
            P.wt("tensor", ACT, ev)
            nrep_last = ev
    es_p0.close()
    P.wt("scalar", LD, consts_ld)
    a_e = P.em("scalar", lambda e: e.activation(out=atmp[:], in_=atmp[:], func=AF.Exp),
               inc=ACT)
    P.wt("vector", ACT, a_e)
    vc = P.em("vector", lambda e: e.tensor_scalar(
        out=atmp[:], in0=atmp[:], scalar1=1.0, scalar2=1e-6, op0=ALU.min,
        op1=ALU.max), inc=DVE)
    P.wt("vector", DVE, vc)
    aneg_ms = P.em("vector", lambda e: e.tensor_scalar(
        out=aneg[:], in0=atmp[:], scalar1=-1.0, scalar2=None, op0=ALU.mult), inc=DVE)
    hcol_ms = P.em("vector", lambda e: e.memset(hcol[:], 0.0), inc=DVE)
    p0_act = ACT.n
    p0_dve = DVE.n
    es_p0x.close()

    es1 = ExitStack()
    x_tm = sb("x_tm", [128, nti, D], BF16, es1)
    sq_b = sb("sq_b", [128, D], BF16, es1)
    xt = sb("xt", [128, D], F32, es1)
    xn_tm = sb("xn_tm", [128, D], BF16, es1)
    z_bf = sb("z_bf", [128, 16, TC], BF16, es1)
    gate_bf = sb("gate_bf", [128, 16, TC], BF16, es1)
    dt_bf = sb("dt_bf", [128, 16, TC], BF16, es1)
    yg_bf = sb("yg_bf", [128, 16, TC], BF16, es1)
    btb = sb("btb", [16, TC], BF16, es1)
    ctb = sb("ctb", [16, TC], BF16, es1)
    bcrep = sb("bcrep", [128, 8, TC], BF16, es1)
    brep4 = bcrep[:, 0:4, :]
    crep4 = bcrep[:, 4:8, :]
    dA4 = sb("dA4", [128, 2, 2, TC + 2], F32, es1)
    U4 = sb("U4", [128, 4, TC + 2], BF16, es1)
    H4 = sb("H4", [128, 2, 4, TC + 2], BF16, es1)
    t1r = sb("t1r", [128, 2, TC], BF16, es1)
    t2r = sb("t2r", [128, 2, TC], BF16, es1)
    sml = sb("sml", [128, 88], F32, es1)
    RW = sb("RW", [128, 3, 8192], BF16, es1)
    xf = sb("xf", [128, 8, TC], BF16, es1)
    scaled = sb("scaled", [128, 8, TC], BF16, es1)
    rep2 = sb("rep2", [128, 1, TC], BF16, es1)
    ffn_acc = sb("ffn_acc", [128, 8, TC], BF16, es1)
    th2 = sb("th2", [128, 2, TC], BF16, es1)
    etmp = sb("etmp", [128, 2, TC], BF16, es1)
    out_tm = sb("out_tm", [128, D], F32, es1)
    comb_fm = sb("comb_fm", [8, TC], BF16, es1)
    comb_tm = sb("comb_tm", [128, 4 * 8], F32, es1)
    comb_bf = sb("comb_bf", [128, 4 * 8], BF16, es1)
    lgq = sb("lgq", [128, 8], F32, es1)
    prb = sb("prb", [128, 8], F32, es1)
    is1 = sb("is1", [128, 8], F32, es1)

    mo_fm = scaled
    shx = scaled
    sgv = btb[0:1, :]
    xn_fm = dA4[:].rearrange("p h s t -> p (h s t)").bitcast(BF16)[
        :, 0:8 * TC].rearrange("p (kb n) -> p kb n", n=TC)
    tmp_e = sq_b[:].bitcast(F32)
    xt_bf = xt[:].bitcast(BF16)
    x2slot = [xt_bf[:, 0:D], xt_bf[:, D:2 * D]]
    tmp_l3 = xt[:].rearrange("p (sl n) -> p sl n", sl=2)
    x3_tm = th2[:].rearrange("p a b -> p (a b)")

    es1p = ExitStack()
    ps = [psum(f"ps{i}", [128, 512], F32, es1p) for i in range(8)]
    psT = [ps[6][:, 0:64].bitcast(BF16), ps[7][:, 0:64].bitcast(BF16)]

    bev = {i: (ACT, 0) for i in range(8)}

    def use_bank(i):
        sem, v = bev[i]
        P.wt("tensor", sem, v)

    def set_bank(i, sem, v):
        bev[i] = (sem, v)

    ring_cons = [0, 0, 0]
    ring_i = [0]

    def wreq(src_fn, stg):
        i = ring_i[0] % 3
        ring_i[0] += 1
        P.wt("sync", PE, ring_cons[i])
        P.wt("sync", STG, stage_ms[stg])
        wv = P.em("sync", lambda e, i=i, src_fn=src_fn: e.dma_start(
            out=RW[:, i, :], in_=src_fn()), inc=W, k=16)
        return [i, wv]

    def wuse(h):
        P.wt("tensor", W, h[1])

    def wview(h, ncols):
        return RW[:, h[0], :].rearrange("p (kb n) -> p kb n", n=ncols)

    def wdone(h, pe_ms):
        ring_cons[h[0]] = pe_ms

    sub = lambda t, i: (lambda: t[:, i * 8192:(i + 1) * 8192])
    win_src = lambda hh: sub(ws_in, hh)
    gate_src = lambda hh: sub(ws_gate, hh)
    dt_src = lambda q: sub(ws_dt, q)
    wout_src = lambda hh: sub(ws_out, hh)
    w1_src = lambda e_: sub(ws_w1, e_)
    w2_src = lambda e_: sub(ws_w2, e_)
    sgu_src = lambda hh: sub(ws_sgu, hh)
    sdn_src = sub(ws_sdn, 0)

    def ln_group(src_fns, cols_list):
        n = len(src_fns)
        r1s, sqs, v2s, v1s, v2ms, v3s, a2s, v4s = ([None] * n for _ in range(8))
        for i in range(n):
            r1s[i] = P.em("vector", lambda e, i=i: e.reduce_sum(
                out=sml[:, cols_list[i]:cols_list[i] + 1], in_=src_fns[i](),
                axis=AX.X), inc=DVE)
        for i in range(n):
            if i > 0:
                P.wt("scalar", DVE, v2s[i - 1])
            sqs[i] = P.em("scalar", lambda e, i=i: e.activation(
                out=sq_b[:], in_=src_fns[i](), func=AF.Square), inc=ACT)
            P.wt("vector", ACT, sqs[i])
            v2s[i] = P.em("vector", lambda e, i=i: e.reduce_sum(
                out=sml[:, cols_list[i] + 1:cols_list[i] + 2], in_=sq_b[:],
                axis=AX.X), inc=DVE)
        for i in range(n):
            P.wt("vector", DVE, r1s[i])
            v1s[i] = P.em("vector", lambda e, i=i: e.tensor_scalar(
                out=sml[:, cols_list[i] + 2:cols_list[i] + 3],
                in0=sml[:, cols_list[i]:cols_list[i] + 1],
                scalar1=-1.0 / D, scalar2=None, op0=ALU.mult), inc=DVE)
        for i in range(n):
            P.wt("vector", DVE, v1s[i])
            v2ms[i] = P.em("vector", lambda e, i=i: e.scalar_tensor_tensor(
                out=sml[:, cols_list[i] + 3:cols_list[i] + 4],
                in0=sml[:, cols_list[i] + 2:cols_list[i] + 3],
                scalar=-1.0, op0=ALU.mult,
                in1=sml[:, cols_list[i] + 2:cols_list[i] + 3],
                op1=ALU.mult), inc=DVE)
        for i in range(n):
            P.wt("vector", DVE, v2ms[i])
            P.wt("vector", DVE, v2s[i])
            v3s[i] = P.em("vector", lambda e, i=i: e.scalar_tensor_tensor(
                out=sml[:, cols_list[i] + 4:cols_list[i] + 5],
                in0=sml[:, cols_list[i] + 1:cols_list[i] + 2],
                scalar=1.0 / D, op0=ALU.mult,
                in1=sml[:, cols_list[i] + 3:cols_list[i] + 4],
                op1=ALU.add), inc=DVE)
        for i in range(n):
            P.wt("scalar", DVE, v3s[i])
            aL = P.em("scalar", lambda e, i=i: e.activation(
                out=sml[:, cols_list[i] + 5:cols_list[i] + 6],
                in_=sml[:, cols_list[i] + 4:cols_list[i] + 5],
                func=AF.Ln, bias=EPS), inc=ACT)
            P.wt("scalar", ACT, aL)
            a2s[i] = P.em("scalar", lambda e, i=i: e.activation(
                out=sml[:, cols_list[i] + 6:cols_list[i] + 7],
                in_=sml[:, cols_list[i] + 5:cols_list[i] + 6],
                func=AF.Exp, scale=-0.5), inc=ACT)
        for i in range(n):
            P.wt("vector", ACT, a2s[i])
            v4s[i] = P.em("vector", lambda e, i=i: e.tensor_tensor(
                out=sml[:, cols_list[i] + 7:cols_list[i] + 8],
                in0=sml[:, cols_list[i] + 2:cols_list[i] + 3],
                in1=sml[:, cols_list[i] + 6:cols_list[i] + 7],
                op=ALU.mult), inc=DVE)
        return a2s, v4s

    def ln_core(src_fn, cols, out_fn):
        a2s, v4s = ln_group([src_fn], [cols])
        P.wt("scalar", DVE, v4s[0])
        P.wt("scalar", ACT, a2s[0])
        return out_fn(cols + 6, cols + 7), 0

    st = dict(u_last=0, ygfin=0, rep_pe=0, ln2_act=0, xn_dve=0, x2_dve=0, xn_tp=0, gate_ev=0,
              xf_tp=0, xf_act=0, sdn_pe=0, ln3_tp=0, resid_x2=0, bc_ev=0,
              dt_clip15=0, gate_pe=0, scmul_last=0, gu_x2=0, x3_stats_dve=0,
              mo_ev=[0] * 8, out_ev=0, wo_pe=0, ygfin_prev=0)
    xslot_pe = [0, 0]
    rep_cons = [0, 0]
    th_cons = [0, 0]
    etmp_cons = [0, 0]
    u_ms_i = {}
    pool_ms = {}

    def ffn_segments(kk):
        segs = []
        fst = dict(comb_done=0, cast_ms=0, rep_ev=[0, 0], silu=[0, 0],
                   scm=[0, 0], fev=[0, 0], w1h=None, w2h=None, sguh=[None, None],
                   sdnh=None, last_w2=0, sgv_act=0, sgrep_ev=0, shx_x2=0,
                   gu_pe=0, ffn_add=st["resid_x2"], scmul=[0] * 8)

        def dv(fn):
            vv = P.em("vector", fn, inc=DVE)
            P.wt("vector", DVE, vv)
            return vv

        def seg_router(g):
            P.wt("tensor", STG, rw_ld)
            P.wt("tensor", ACT, st["xf_act"])
            use_bank(0)
            if fst["comb_done"]:
                P.wt("tensor", DVE, fst["comb_done"])
            last_mmr = None
            for kb in range(8):
                last_mmr = P.em("tensor", lambda e, g=g, kb=kb: e.matmul(
                    ps[0][:, 0:8], xf[:, kb, g * 128:(g + 1) * 128],
                    rwb[:, kb * 8:(kb + 1) * 8], start=(kb == 0), stop=(kb == 7)),
                    inc=(PE if kb == 7 else None))
            P.wt("vector", PE, last_mmr)
            lg_ms = dv(lambda e: e.tensor_copy(out=lgq[:, :], in_=ps[0][:, 0:8]))
            set_bank(0, DVE, lg_ms)
            v = dv(lambda e: e.reduce_max(out=sml[:, 72:73], in_=lgq[:, :], axis=AX.X))
            v = dv(lambda e: e.tensor_scalar(
                out=sml[:, 72:73], in0=sml[:, 72:73], scalar1=-1.0, scalar2=None,
                op0=ALU.mult))
            P.wt("scalar", DVE, v)
            aq = P.em("scalar", lambda e: e.activation(
                out=prb[:, :], in_=lgq[:, :], func=AF.Exp, bias=sml[:, 72:73]),
                inc=ACT)
            P.wt("vector", ACT, aq)
            v = dv(lambda e: e.reduce_sum(out=sml[:, 73:74], in_=prb[:, :], axis=AX.X))
            P.wt("scalar", DVE, v)
            aL2 = P.em("scalar", lambda e: e.activation(
                out=sml[:, 74:75], in_=sml[:, 73:74], func=AF.Ln), inc=ACT)
            P.wt("scalar", ACT, aL2)
            aq = P.em("scalar", lambda e: e.activation(
                out=sml[:, 75:76], in_=sml[:, 74:75], func=AF.Exp, scale=-1.0),
                inc=ACT)
            P.wt("vector", ACT, aq)
            dv(lambda e: e.tensor_scalar(
                out=prb[:, :], in0=prb[:, :], scalar1=sml[:, 75:76], scalar2=None,
                op0=ALU.mult))
            dv(lambda e: e.reduce_max(out=sml[:, 76:77], in_=prb[:, :], axis=AX.X))
            dv(lambda e: e.tensor_scalar(
                out=is1[:, :], in0=prb[:, :], scalar1=sml[:, 76:77], scalar2=None,
                op0=ALU.is_ge))
            dv(lambda e: e.reduce_sum(out=sml[:, 77:78], in_=is1[:, :], axis=AX.X))
            dv(lambda e: e.scalar_tensor_tensor(
                out=is1[:, :], in0=is1[:, :], scalar=-2.0, op0=ALU.mult,
                in1=prb[:, :], op1=ALU.add))
            dv(lambda e: e.reduce_max(out=sml[:, 78:79], in_=is1[:, :], axis=AX.X))
            dv(lambda e: e.tensor_scalar(
                out=sml[:, 77:78], in0=sml[:, 77:78], scalar1=1.5, scalar2=None,
                op0=ALU.is_ge))
            dv(lambda e: e.tensor_tensor(
                out=sml[:, 79:80], in0=sml[:, 76:77], in1=sml[:, 78:79],
                op=ALU.subtract))
            dv(lambda e: e.tensor_tensor(
                out=sml[:, 79:80], in0=sml[:, 77:78], in1=sml[:, 79:80],
                op=ALU.mult))
            dv(lambda e: e.tensor_tensor(
                out=sml[:, 79:80], in0=sml[:, 78:79], in1=sml[:, 79:80],
                op=ALU.add))
            dv(lambda e: e.tensor_scalar(
                out=is1[:, :], in0=prb[:, :], scalar1=sml[:, 79:80], scalar2=None,
                op0=ALU.is_ge))
            fst["comb_done"] = dv(lambda e, g=g: e.tensor_tensor(
                out=comb_tm[:, g * 8:(g + 1) * 8], in0=prb[:, :], in1=is1[:, :],
                op=ALU.mult))

        def seg_comb():
            fst["cast_ms"] = P.em("vector", lambda e: e.tensor_copy(
                out=comb_bf[:, :], in_=comb_tm[:, :]), inc=DVE)
            if dump and kk == dumpc:
                P.wt("sync", DVE, fst["cast_ms"])
                P.em("sync", lambda e: e.dma_start(
                    out=dumps["d_comb"][:, :], in_=comb_tm[:, :]), inc=ST, k=16)
                dump_barrier()
            prev = 0
            for g in range(nti):
                sl = g % 2
                P.wt("tensor", DVE, fst["cast_ms"])
                use_bank(6 + sl)
                if prev:
                    P.wt("tensor", DVE, prev)
                tp = P.em("tensor", lambda e, g=g, sl=sl: e.transpose(
                    psT[sl][0:8, :], comb_bf[:, g * 8:(g + 1) * 8], ident_bf[:]),
                    inc=PE)
                P.wt("vector", PE, tp)
                prev = P.em("vector", lambda e, g=g, sl=sl: e.tensor_copy(
                    out=comb_fm[:, g * 128:(g + 1) * 128], in_=psT[sl][0:8, :]),
                    inc=DVE)
                set_bank(6 + sl, DVE, prev)
            fst["comb_fm_done"] = prev

        def seg_w1(e_, half):
            if e_ == 0 and half == 0:
                fst["w1h"] = wreq(w1_src(0), "w1")
                fst["w2h"] = wreq(w2_src(0), "w2")
            sl = e_ % 2
            if half == 0:
                use_bank(sl)
                P.wt("tensor", DVE, fst["comb_fm_done"])
                mm = P.em("tensor", lambda e, e_=e_, sl=sl: e.matmul(
                    ps[sl][:, :], oh8[:, e_ * 128:(e_ + 1) * 128], comb_fm[:, :],
                    start=True, stop=True), inc=PE)
                P.wt("scalar", PE, mm)
                P.wt("scalar", X2, rep_cons[0])
                rep_ev = P.em("scalar", lambda e, e_=e_, sl=sl: e.activation(
                    out=rep2[:, 0, :], in_=ps[sl][:, :], func=AF.Copy), inc=ACT)
                set_bank(sl, ACT, rep_ev)
                fst["rep_ev"][sl] = rep_ev
            w1h = fst["w1h"]
            wuse(w1h)
            w1v = wview(w1h, D)
            last_p = None
            for m in range(4 * half, 4 * half + 4):
                bsl = 4 + m % 2
                use_bank(bsl)
                for kb in range(8):
                    last_p = P.em("tensor", lambda e, m=m, kb=kb, w1v=w1v, bsl=bsl:
                                  e.matmul(ps[bsl][:, :],
                                           w1v[:, kb, m * 128:(m + 1) * 128],
                                           xf[:, kb, :], start=(kb == 0),
                                           stop=(kb == 7)),
                                  inc=(PE if kb == 7 else None))
                P.wt("scalar", PE, last_p)
                if e_ == 0 and m == 0:
                    P.wt("scalar", DVE, st["x2_dve"])
                    P.wt("scalar", PE, st["sdn_pe"])
                sil = P.em("scalar", lambda e, m=m, bsl=bsl: e.activation(
                    out=scaled[:, m, :], in_=ps[bsl][:, :], func=AF.Silu), inc=ACT)
                set_bank(bsl, ACT, sil)
                P.wt("gpsimd", ACT, sil)
                P.wt("gpsimd", ACT, fst["rep_ev"][sl])
                scm = P.em("gpsimd", lambda e, m=m:
                           e.tensor_tensor(out=scaled[:, m, :], in0=scaled[:, m, :],
                                           in1=rep2[:, 0, :], op=ALU.mult), inc=X2)
                fst["scmul"][m] = scm
            if half == 1:
                rep_cons[0] = fst["scmul"][7]
                fst["scm_last"] = fst["scmul"][7]
                wdone(w1h, last_p)
                fst["w1h"] = wreq(w1_src(e_ + 1), "w1") if e_ < NE - 1 else None

        def seg_w2(e_, half):
            w2h = fst["w2h"]
            wuse(w2h)
            w2v = wview(w2h, D)
            P.wt("tensor", X2, fst["scm_last"])
            last_w2 = None
            for m in range(4 * half, 4 * half + 4):
                fsl = 2 + m % 2
                use_bank(fsl)
                for kb in range(8):
                    last_w2 = P.em("tensor", lambda e, m=m, kb=kb, w2v=w2v, fsl=fsl:
                                   e.matmul(ps[fsl][:, :],
                                            w2v[:, kb, m * 128:(m + 1) * 128],
                                            scaled[:, kb, :], start=(kb == 0),
                                            stop=(kb == 7)),
                                   inc=(PE if kb == 7 else None))
                P.wt("scalar", PE, last_w2)
                if e_ == 0:
                    if m < 2:
                        P.wt("scalar", PE, st["ln3_tp"])
                        P.wt("scalar", X2, st["resid_x2"])
                    ev = P.em("scalar", lambda e, m=m, fsl=fsl: e.activation(
                        out=ffn_acc[:, m, :], in_=ps[fsl][:, :], func=AF.Copy),
                        inc=ACT)
                else:
                    esl = m % 2
                    P.wt("scalar", X2, etmp_cons[esl])
                    ev = P.em("scalar", lambda e, m=m, fsl=fsl, esl=esl: e.activation(
                        out=etmp[:, esl, :], in_=ps[fsl][:, :], func=AF.Copy),
                        inc=ACT)
                    P.wt("gpsimd", ACT, ev)
                    ad = P.em("gpsimd", lambda e, m=m, esl=esl: e.tensor_tensor(
                        out=ffn_acc[:, m, :], in0=ffn_acc[:, m, :],
                        in1=etmp[:, esl, :], op=ALU.add), inc=X2)
                    etmp_cons[esl] = ad
                    fst["ffn_add"] = ad
                set_bank(fsl, ACT, ev)
                fst["fev"][m % 2] = ev
            if half == 1:
                wdone(w2h, last_w2)
                fst["last_w2"] = last_w2
                if e_ < NE - 1:
                    fst["w2h"] = wreq(w2_src(e_ + 1), "w2")
                else:
                    fst["sguh"][0] = wreq(sgu_src(0), "sgu")
                    fst["sguh"][1] = wreq(sgu_src(1), "sgu")

        def seg_sgate():
            P.wt("tensor", STG, sgw_ld)
            use_bank(5)
            last_mmr = None
            for kb in range(8):
                last_mmr = P.em("tensor", lambda e, kb=kb: e.matmul(
                    ps[5][0:1, :], sgwb[:, kb:kb + 1], xf[:, kb, :],
                    start=(kb == 0), stop=(kb == 7)), inc=(PE if kb == 7 else None))
            P.wt("scalar", PE, last_mmr)
            P.wt("scalar", PE, st["rep_pe"])
            asg = P.em("scalar", lambda e: e.activation(
                out=sgv, in_=ps[5][0:1, :], func=AF.Sigmoid), inc=ACT)
            set_bank(5, ACT, asg)
            fst["sgv_act"] = asg
            P.wt("tensor", ACT, asg)
            use_bank(2)
            mm = P.em("tensor", lambda e: e.matmul(
                ps[2][:, :], ones1b[:, :], sgv, start=True, stop=True), inc=PE)
            P.wt("scalar", PE, mm)
            P.wt("scalar", X2, rep_cons[0])
            P.wt("scalar", X2, rep_cons[1])
            ev = P.em("scalar", lambda e: e.activation(
                out=rep2[:, 0, :], in_=ps[2][:, :], func=AF.Copy), inc=ACT)
            set_bank(2, ACT, ev)
            fst["sgrep_ev"] = ev

        def seg_gu(mg):
            sguv0 = wview(fst["sguh"][0], DI)
            sguv1 = wview(fst["sguh"][1], DI)
            wuse(fst["sguh"][0])
            wuse(fst["sguh"][1])
            use_bank(4)
            gumm = None
            for kb in range(8):
                v0 = sguv0 if kb < 4 else sguv1
                P.em("tensor", lambda e, mg=mg, kb=kb, v0=v0: e.matmul(
                    ps[4][:, :], v0[:, kb % 4, mg * 128:(mg + 1) * 128],
                    xf[:, kb, :], start=(kb == 0), stop=(kb == 7)),
                    inc=(PE if kb == 7 else None))
            use_bank(5)
            for kb in range(8):
                v0 = sguv0 if kb < 4 else sguv1
                gumm = P.em("tensor", lambda e, mg=mg, kb=kb, v0=v0: e.matmul(
                    ps[5][:, :], v0[:, kb % 4, (mg + 8) * 128:(mg + 9) * 128],
                    xf[:, kb, :], start=(kb == 0), stop=(kb == 7)),
                    inc=(PE if kb == 7 else None))
            P.wt("scalar", PE, gumm)
            P.wt("scalar", X2, th_cons[0])
            if mg == 0:
                P.wt("scalar", DVE, st["x3_stats_dve"])
            ag = P.em("scalar", lambda e: e.activation(
                out=th2[:, 0, :], in_=ps[4][:, :], func=AF.Silu), inc=ACT)
            set_bank(4, ACT, ag)
            P.wt("scalar", X2, th_cons[1])
            au = P.em("scalar", lambda e: e.activation(
                out=th2[:, 1, :], in_=ps[5][:, :], func=AF.Copy), inc=ACT)
            set_bank(5, ACT, au)
            P.wt("gpsimd", ACT, au)
            P.wt("gpsimd", ACT, fst["sgrep_ev"])
            if mg == 0:
                P.wt("gpsimd", PE, fst["last_w2"])
            m1 = P.em("gpsimd", lambda e: e.tensor_tensor(
                out=th2[:, 1, :], in0=th2[:, 0, :], in1=th2[:, 1, :],
                op=ALU.mult), inc=X2)
            th_cons[0] = m1
            m2 = P.em("gpsimd", lambda e, mg=mg: e.tensor_tensor(
                out=shx[:, mg, :], in0=th2[:, 1, :], in1=rep2[:, 0, :],
                op=ALU.mult), inc=X2)
            th_cons[1] = m2
            fst["shx_x2"] = m2
            if mg == NE - 1:
                rep_cons[0] = m2
                wdone(fst["sguh"][0], gumm)
                wdone(fst["sguh"][1], gumm)
                fst["gu_pe"] = gumm
                fst["sdnh"] = wreq(sdn_src, "sdn")
                if dump and kk == dumpc:
                    P.wt("sync", X2, m2)
                    P.em("sync", lambda e: e.dma_start(
                        out=dumps["d_shx"][:, :], in_=shx[:, :, :]), inc=ST, k=16)
                    dump_barrier()

        def seg_sdn():
            sdv = wview(fst["sdnh"], D)
            wuse(fst["sdnh"])
            P.wt("tensor", X2, fst["shx_x2"])
            sdmm = None
            for m in range(8):
                fsl = 2 + m % 2
                use_bank(fsl)
                for kb in range(8):
                    sdmm = P.em("tensor", lambda e, m=m, kb=kb, sdv=sdv, fsl=fsl:
                                e.matmul(ps[fsl][:, :],
                                         sdv[:, kb, m * 128:(m + 1) * 128],
                                         shx[:, kb, :], start=(kb == 0),
                                         stop=(kb == 7)),
                                inc=(PE if kb == 7 else None))
                P.wt("scalar", PE, sdmm)
                esl = m % 2
                P.wt("scalar", X2, etmp_cons[esl])
                ev = P.em("scalar", lambda e, m=m, fsl=fsl, esl=esl: e.activation(
                    out=etmp[:, esl, :], in_=ps[fsl][:, :], func=AF.Copy), inc=ACT)
                set_bank(fsl, ACT, ev)
                P.wt("gpsimd", ACT, ev)
                ad = P.em("gpsimd", lambda e, m=m, esl=esl: e.tensor_tensor(
                    out=ffn_acc[:, m, :], in0=ffn_acc[:, m, :],
                    in1=etmp[:, esl, :], op=ALU.add), inc=X2)
                etmp_cons[esl] = ad
                fst["ffn_add"] = ad
            st["sdn_pe"] = sdmm
            wdone(fst["sdnh"], sdmm)
            if dump and kk == dumpc:
                P.wt("sync", X2, fst["ffn_add"])
                P.em("sync", lambda e: e.dma_start(
                    out=dumps["d_ffn"][:, :], in_=ffn_acc[:, :, :]), inc=ST, k=16)
                dump_barrier()

        def seg_resid():
            ad = 0
            for m in range(8):
                ad = P.em("gpsimd", lambda e, m=m: e.tensor_tensor(
                    out=ffn_acc[:, m, :], in0=ffn_acc[:, m, :], in1=xf[:, m, :],
                    op=ALU.add), inc=X2)
            st["resid_x2"] = ad
            fst["ffn_add"] = ad

        def seg_ln3(g):
            for m in range(8):
                sl = m % 2
                P.wt("tensor", X2, fst["ffn_add"])
                use_bank(6 + sl)
                tp = P.em("tensor", lambda e, g=g, m=m, sl=sl: e.transpose(
                    psT[sl][:, :], ffn_acc[:, m, g * 128:(g + 1) * 128],
                    ident_bf[:]), inc=PE)
                st["ln3_tp"] = tp
                P.wt("scalar", PE, tp)
                if m < 2:
                    P.wt("scalar", X2, th_cons[0])
                    P.wt("scalar", X2, th_cons[1])
                    P.wt("scalar", DVE, st["x3_stats_dve"])
                ev = P.em("scalar", lambda e, m=m, sl=sl: e.activation(
                    out=x3_tm[:, m * 128:(m + 1) * 128], in_=psT[sl][:, :],
                    func=AF.Copy), inc=ACT)
                set_bank(6 + sl, ACT, ev)
            x3_ev = ev

            def ln3_out(sc_col, nb_col, g=g):
                P.wt("scalar", ST, st["out_ev"])
                P.wt("scalar", ACT, x3_ev)
                a3 = P.em("scalar", lambda e, sc_col=sc_col, nb_col=nb_col:
                          e.activation(out=out_tm[:, :], in_=x3_tm[:, :],
                                       func=AF.Identity,
                                       scale=sml[:, sc_col:sc_col + 1],
                                       bias=sml[:, nb_col:nb_col + 1]), inc=ACT)
                return a3
            P.wt("vector", ACT, x3_ev)
            a3, v2ms = ln_core(lambda: x3_tm[:, :], 64, ln3_out)
            st["x3_stats_dve"] = v2ms
            th_cons[0] = 0
            th_cons[1] = 0
            P.wt("sync", ACT, a3)
            stv = P.em("sync", lambda e, g=g: e.dma_start(
                out=out_d[kk * TC + g * 128: kk * TC + (g + 1) * 128, :],
                in_=out_tm[:, :]), inc=ST, k=16)
            st["out_ev"] = stv

        for g in range(nti):
            segs.append(lambda g=g: seg_router(g))
        segs.append(seg_comb)
        for e_ in range(NE):
            segs.append(lambda e_=e_: seg_w1(e_, 0))
            segs.append(lambda e_=e_: seg_w1(e_, 1))
            segs.append(lambda e_=e_: seg_w2(e_, 0))
            segs.append(lambda e_=e_: seg_w2(e_, 1))
        segs.append(seg_sgate)
        for mg in range(8):
            segs.append(lambda mg=mg: seg_gu(mg))
        segs.append(seg_sdn)
        segs.append(seg_resid)
        for g in range(nti):
            segs.append(lambda g=g: seg_ln3(g))
        return segs

    if sched is None:
        base = [11 + 5 * e for e in range(8)]
        sched = ([1, 2, 3, 4, 5] +
                 sum([[b, b + 1, b + 2, b + 3] for b in base], []) +
                 [51] + [52 + mg for mg in range(8)] +
                 [60, 61] + [62, 63, 99, 99])
    try:
        segs = []
        nseg = [0]
        for k in range(nch + 1):
            if k > 0:
                kk = k - 1
                wo0 = wreq(wout_src(0), "out")
                wo1 = wreq(wout_src(1), "out")
                P.wt("tensor", DVE, st["ygfin"])
                womm = None
                for m in range(8):
                    sl = m % 2
                    use_bank(sl)
                    wuse(wo0)
                    wuse(wo1)
                    wv0 = wview(wo0, D)
                    wv1 = wview(wo1, D)
                    for kb in range(16):
                        wv = wv0 if kb < 8 else wv1
                        womm = P.em("tensor", lambda e, m=m, kb=kb, wv=wv, sl=sl:
                                    e.matmul(ps[sl][:, :],
                                             wv[:, kb % 8, m * 128:(m + 1) * 128],
                                             yg_bf[:, kb, :], start=(kb == 0),
                                             stop=(kb == 15)),
                                    inc=(PE if kb == 15 else None))
                    P.wt("scalar", PE, womm)
                    if m < 2:
                        P.wt("scalar", PE, st["sdn_pe"])
                        P.wt("scalar", X2, st["gu_x2"])
                    ev = P.em("scalar", lambda e, m=m, sl=sl: e.activation(
                        out=mo_fm[:, m, :], in_=ps[sl][:, :], func=AF.Copy),
                        inc=ACT)
                    set_bank(sl, ACT, ev)
                    st["mo_ev"][m] = ev
                wdone(wo0, womm)
                wdone(wo1, womm)
                st["wo_pe"] = womm
                if dump and kk == dumpc:
                    P.wt("sync", ACT, st["mo_ev"][7])
                    P.em("sync", lambda e: e.dma_start(
                        out=dumps["d_mo"][:, :], in_=mo_fm[:, :, :]), inc=ST, k=16)
                    dump_barrier()
                ckpt("wout")
                x2v = 0
                for i in range(nti):
                    for m in range(8):
                        sl = m % 2
                        P.wt("tensor", ACT, st["mo_ev"][m])
                        use_bank(6 + sl)
                        tp = P.em("tensor", lambda e, i=i, m=m, sl=sl: e.transpose(
                            psT[sl][:, :], mo_fm[:, m, i * 128:(i + 1) * 128],
                            ident_bf[:]), inc=PE)
                        P.wt("vector", PE, tp)
                        x2v = P.em("vector", lambda e, i=i, m=m, sl=sl:
                                   e.scalar_tensor_tensor(
                            out=x_tm[:, i, m * 128:(m + 1) * 128],
                            in0=x_tm[:, i, m * 128:(m + 1) * 128], scalar=2.0,
                            op0=ALU.mult, in1=psT[sl][:, :], op1=ALU.add), inc=DVE)
                        set_bank(6 + sl, DVE, x2v)
                st["x2_dve"] = x2v
                ckpt("x2")
                xf_act = 0
                P.wt("vector", DVE, st["x2_dve"])
                P.wt("scalar", DVE, st["x2_dve"])
                ln2_a2, ln2_v4 = ln_group(
                    [(lambda i=i: x_tm[:, i, :]) for i in range(nti)],
                    [32 + 8 * i for i in range(nti)])
                for i in range(nti):
                    isl = i % 2
                    sc_col, nb_col = 32 + 8 * i + 6, 32 + 8 * i + 7
                    P.wt("scalar", DVE, ln2_v4[i])
                    P.wt("scalar", ACT, ln2_a2[i])
                    P.wt("scalar", PE, xslot_pe[isl])
                    a3 = P.em("scalar", lambda e, i=i, isl=isl, sc_col=sc_col,
                              nb_col=nb_col: e.activation(
                        out=x2slot[isl], in_=x_tm[:, i, :], func=AF.Identity,
                        scale=sml[:, sc_col:sc_col + 1],
                        bias=sml[:, nb_col:nb_col + 1]), inc=ACT)
                    st["ln2_act"] = a3
                    for kb in range(8):
                        sl = kb % 2
                        P.wt("tensor", ACT, a3)
                        use_bank(6 + sl)
                        tp = P.em("tensor", lambda e, i=i, kb=kb, sl=sl, isl=isl:
                                  e.transpose(psT[sl][:, :],
                                              x2slot[isl][:, kb * 128:(kb + 1) * 128],
                                              ident_bf[:]), inc=PE)
                        xslot_pe[isl] = tp
                        st["xf_tp"] = tp
                        P.wt("scalar", PE, tp)
                        P.wt("scalar", X2, st["resid_x2"])
                        xf_act = P.em("scalar", lambda e, i=i, kb=kb, sl=sl:
                                      e.activation(
                            out=xf[:, kb, i * 128:(i + 1) * 128],
                            in_=psT[sl][:, :], func=AF.Copy), inc=ACT)
                        set_bank(6 + sl, ACT, xf_act)
                st["xf_act"] = xf_act
                if dump and kk == dumpc:
                    P.wt("sync", ACT, xf_act)
                    P.em("sync", lambda e: e.dma_start(
                        out=dumps["d_xf"][:, :], in_=xf[:, :, :]), inc=ST, k=16)
                    dump_barrier()
                segs = ffn_segments(kk)
                nseg[0] = 0
                ckpt("ln2")

            def pump(upto_it):
                while nseg[0] < len(segs) and sched[nseg[0]] <= upto_it:
                    segs[nseg[0]]()
                    nseg[0] += 1

            def drain():
                while nseg[0] < len(segs):
                    segs[nseg[0]]()
                    nseg[0] += 1

            if k == nch:
                drain()
                break

            c = k
            if c == 0:
                P.wt("gpsimd", ACT, p0_act)
                P.wt("gpsimd", DVE, p0_dve)
            else:
                P.wt("gpsimd", ACT, st["ln2_act"])
            xv = P.em("gpsimd", lambda e, c=c: e.dma_start(
                out=x_tm[:, :, :],
                in_=x_d[c * TC:(c + 1) * TC, :].rearrange("(i p) d -> p i d", p=128)),
                inc=X, k=16)

            xn_ev_last = 0
            h_win0 = wreq(win_src(0), "in")
            h_win1 = wreq(win_src(1), "in")
            P.wt("vector", X, xv)
            if c == 0:
                P.wt("vector", ACT, p0_act)
                P.wt("vector", DVE, p0_dve)
            P.wt("scalar", X, xv)
            if c == 0:
                P.wt("scalar", ACT, p0_act)
            ln1_a2, ln1_v4 = ln_group(
                [(lambda i=i: x_tm[:, i, :]) for i in range(nti)],
                [8 * i for i in range(nti)])
            for i in range(nti):
                sc_col, nb_col = 8 * i + 6, 8 * i + 7
                P.wt("scalar", DVE, ln1_v4[i])
                P.wt("scalar", ACT, ln1_a2[i])
                P.wt("scalar", PE, st["xf_tp"])
                a3 = P.em("scalar", lambda e, i=i, sc_col=sc_col, nb_col=nb_col:
                          e.activation(out=xt_bf[:, 0:D], in_=x_tm[:, i, :],
                                       func=AF.Identity,
                                       scale=sml[:, sc_col:sc_col + 1],
                                       bias=sml[:, nb_col:nb_col + 1]), inc=ACT)
                P.wt("vector", ACT, a3)
                vm = P.em("vector", lambda e: e.tensor_tensor(
                    out=xt_bf[:, D:2 * D], in0=xt_bf[:, 0:D], in1=nw_rep[:],
                    op=ALU.mult), inc=DVE)
                P.wt("vector", DVE, vm)
                P.wt("vector", PE, st["xn_tp"])
                v4 = P.em("vector", lambda e: e.tensor_tensor(
                    out=xn_tm[:], in0=xt_bf[:, D:2 * D], in1=nb_rep[:],
                    op=ALU.add), inc=DVE)
                st["xn_dve"] = v4
                P.wt("tensor", DVE, v4)
                if c == 0 and i == 0:
                    P.wt("tensor", LD, consts_ld)
                for kb in range(8):
                    sl = kb % 2
                    use_bank(6 + sl)
                    tp = P.em("tensor", lambda e, kb=kb, sl=sl: e.transpose(
                        psT[sl][:, :], xn_tm[:, kb * 128:(kb + 1) * 128],
                        ident_bf[:]), inc=PE)
                    st["xn_tp"] = tp
                    P.wt("scalar", PE, tp)
                    if i == 0 and kb == 0:
                        P.wt("scalar", SCAN, 128 * c)
                    ev = P.em("scalar", lambda e, kb=kb, i=i, sl=sl: e.activation(
                        out=xn_fm[:, kb, i * 128:(i + 1) * 128], in_=psT[sl][:, :],
                        func=AF.Copy), inc=ACT)
                    set_bank(6 + sl, ACT, ev)
                    xn_ev_last = ev
            if dump and c == dumpc:
                P.wt("sync", ACT, xn_ev_last)
                P.em("sync", lambda e: e.dma_start(
                    out=dumps["d_xn"][:, :], in_=xn_fm[:, :, :]), inc=ST, k=16)
                dump_barrier()
            ckpt("ln1")

            h_g0 = wreq(gate_src(0), "gate")
            P.wt("tensor", ACT, xn_ev_last)
            wuse(h_win0)
            wuse(h_win1)
            wiv0 = wview(h_win0, DI)
            wiv1 = wview(h_win1, DI)
            zmm = None
            for m in range(16):
                sl = m % 2
                use_bank(sl)
                for kb in range(8):
                    wv = wiv0 if kb < 4 else wiv1
                    zmm = P.em("tensor", lambda e, m=m, kb=kb, wv=wv, sl=sl:
                               e.matmul(ps[sl][:, :],
                                        wv[:, kb % 4, m * 128:(m + 1) * 128],
                                        xn_fm[:, kb, :], start=(kb == 0),
                                        stop=(kb == 7)),
                               inc=(PE if kb == 7 else None))
                P.wt("scalar", PE, zmm)
                if m == 0:
                    P.wt("scalar", DVE, st["u_last"])
                ev = P.em("scalar", lambda e, m=m, sl=sl: e.activation(
                    out=z_bf[:, m, :], in_=ps[sl][:, :], func=AF.Copy), inc=ACT)
                set_bank(sl, ACT, ev)
            wdone(h_win0, zmm)
            wdone(h_win1, zmm)
            h_g1 = wreq(gate_src(1), "gate")
            ckpt("z")

            wuse(h_g0)
            wuse(h_g1)
            gv0 = wview(h_g0, DI)
            gv1 = wview(h_g1, DI)
            gmm = None
            for m in range(16):
                sl = m % 2
                use_bank(sl)
                for kb in range(8):
                    wv = gv0 if kb < 4 else gv1
                    gmm = P.em("tensor", lambda e, m=m, kb=kb, wv=wv, sl=sl:
                               e.matmul(ps[sl][:, :],
                                        wv[:, kb % 4, m * 128:(m + 1) * 128],
                                        xn_fm[:, kb, :], start=(kb == 0),
                                        stop=(kb == 7)),
                               inc=(PE if kb == 7 else None))
                P.wt("scalar", PE, gmm)
                if m == 0:
                    P.wt("scalar", DVE, st["ygfin"])
                ev = P.em("scalar", lambda e, m=m, sl=sl: e.activation(
                    out=gate_bf[:, m, :], in_=ps[sl][:, :], func=AF.Silu), inc=ACT)
                set_bank(sl, ACT, ev)
                st["gate_ev"] = ev
            wdone(h_g0, gmm)
            wdone(h_g1, gmm)
            st["gate_pe"] = gmm
            P.wt("vector", PE, gmm)
            z0 = P.em("vector", lambda e: e.memset(dA4[:, :, :, 0:2], 0.0), inc=DVE)
            if c == 0:
                u4z_ms = P.em("vector", lambda e: e.memset(U4[:, :, 0:1], 0.0),
                              inc=DVE)
                P.wt("vector", DVE, u4z_ms)
            h_dt = [wreq(dt_src(0), "dt"), wreq(dt_src(1), "dt")]
            ckpt("gate")

            P.wt("tensor", STG, max(wb_ld, wc_ld))
            P.wt("tensor", ACT, st["bc_ev"])
            bc_last = None
            for (pbc, wsb, dstt) in [(ps[2][0:16, :], wbb, btb),
                                     (ps[3][0:16, :], wcb, ctb)]:
                use_bank(2 if dstt is btb else 3)
                last_mm = None
                for kb in range(16):
                    last_mm = P.em("tensor", lambda e, kb=kb, pbc=pbc, wsb=wsb:
                                   e.matmul(pbc[:, :], wsb[:, kb * DS:(kb + 1) * DS],
                                            z_bf[:, kb, :], start=(kb == 0),
                                            stop=(kb == 15)),
                                   inc=(PE if kb == 15 else None))
                bc_last = last_mm
                P.wt("scalar", PE, last_mm)
                P.wt("scalar", PE, st["rep_pe"])
                bc_ev = P.em("scalar", lambda e, pbc=pbc, dstt=dstt: e.activation(
                    out=dstt[:, :], in_=pbc[:, :], func=AF.Copy), inc=ACT)
                set_bank(2 if dstt is btb else 3, ACT, bc_ev)
            st["bc_ev"] = bc_ev

            def rep_build(sgrp):
                brep_last = crep_last = 0
                P.wt("tensor", ACT, st["bc_ev"])
                if sgrp > 0:
                    P.wt("tensor", DVE, u_ms_i.get((sgrp - 1, 15), 0))
                    P.wt("tensor", POOL, pool_ms.get(64 * c + (sgrp - 1) * 16 + 15, 0))
                else:
                    P.wt("tensor", DVE, st["u_last"])
                    P.wt("tensor", POOL, 128 * c)
                mm = None
                for (srct, dstr) in [(btb, brep4), (ctb, crep4)]:
                    for j in range(4):
                        s_ = sgrp * 4 + j
                        sl = (s_ + (0 if srct is btb else 4)) % 2
                        use_bank(sl)
                        mm = P.em("tensor", lambda e, s_=s_, srct=srct, sl=sl:
                                  e.matmul(ps[sl][:, :],
                                           oh16[:, s_ * 128:(s_ + 1) * 128],
                                           srct[:, :], start=True, stop=True),
                                  inc=PE)
                        P.wt("scalar", PE, mm)
                        ev = P.em("scalar", lambda e, j=j, dstr=dstr, sl=sl:
                                  e.activation(out=dstr[:, j, :],
                                               in_=ps[sl][:, :], func=AF.Copy),
                                  inc=ACT)
                        set_bank(sl, ACT, ev)
                        if srct is btb:
                            brep_last = ev
                        else:
                            crep_last = ev
                st["rep_pe"] = mm
                return brep_last, crep_last

            rep_last, crep_last = rep_build(0)

            dt_clip = []
            defer_clips = []
            dt_prev_act = 0
            for pss in range(2):
                for i_ in range(8):
                    use_bank(i_)
                if pss == 0:
                    P.wt("scalar", DVE, st["xn_dve"])
                dtmm = None
                h_q = h_dt if pss == 0 else [wreq(dt_src(0), "dt"),
                                             wreq(dt_src(1), "dt")]

                for q in range(4):
                    if q >= 2:
                        h_q.append(wreq(dt_src(q), "dt"))
                    hq = h_q[q]
                    wuse(hq)
                    qv = wview(hq, DI)
                    last_q = None
                    for kbq in range(4):
                        kb = q * 4 + kbq
                        for m in range(8):
                            last_q = P.em("tensor", lambda e, kb=kb, kbq=kbq, m=m,
                                          qv=qv, pss=pss: e.matmul(
                                ps[m][:, :],
                                qv[:, kbq, (pss * 8 + m) * 128:(pss * 8 + m + 1) * 128],
                                z_bf[:, kb, :], start=(kb == 0), stop=(kb == 15)),
                                inc=(PE if (kbq == 3 and m == 7) else None))
                    wdone(hq, last_q)
                    dtmm = last_q
                P.wt("scalar", PE, dtmm)
                for m in range(8):
                    mm_ = pss * 8 + m
                    P.wt("scalar", DVE, dt_clip[-2] if len(dt_clip) >= 2 else 0)
                    a1 = P.em("scalar", lambda e, m=m, mm_=mm_: e.activation(
                        out=tmp_e[:, :], in_=ps[m][:, :], func=AF.Exp,
                        bias=bdt_sb[:, mm_:mm_ + 1]), inc=ACT)
                    set_bank(m, ACT, a1)
                    P.wt("scalar", ACT, a1)
                    a2 = P.em("scalar", lambda e, m=m: e.activation(
                        out=tmp_l3[:, m % 2, :], in_=tmp_e[:, :], func=AF.Ln,
                        bias=1.0), inc=ACT)
                    if pss == 0:
                        P.wt("vector", ACT, a2)
                        v = P.em("vector", lambda e, m=m, mm_=mm_: e.tensor_scalar(
                            out=dt_bf[:, mm_, :], in0=tmp_l3[:, m % 2, :],
                            scalar1=1.0, scalar2=0.001, op0=ALU.min,
                            op1=ALU.max), inc=DVE)
                        dt_clip.append(v)
                    else:
                        defer_clips.append((m, mm_, a2))
                if pss == 1:
                    P.wt("vector", PE, dtmm)
                    dtz_ms = P.em("vector", lambda e: e.tensor_tensor(
                        out=z_bf[:, 0:8, :], in0=dt_bf[:, 0:8, :],
                        in1=z_bf[:, 0:8, :], op=ALU.mult), inc=DVE)
                    for (m, mm_, a2) in defer_clips:
                        P.wt("vector", ACT, a2)
                        v = P.em("vector", lambda e, m=m, mm_=mm_: e.tensor_scalar(
                            out=dt_bf[:, mm_, :], in0=tmp_l3[:, m % 2, :],
                            scalar1=1.0, scalar2=0.001, op0=ALU.min,
                            op1=ALU.max), inc=DVE)
                        dt_clip.append(v)
            st["dt_clip15"] = dt_clip[15]
            ckpt("dt")

            dtz_hi_emitted = [False]

            def emit_dtz_hi():
                if dtz_hi_emitted[0]:
                    return
                dtz_hi_emitted[0] = True
                P.em("vector", lambda e: e.tensor_tensor(
                    out=z_bf[:, 8:16, :], in0=dt_bf[:, 8:16, :],
                    in1=z_bf[:, 8:16, :], op=ALU.mult), inc=DVE)
            if dump:
                emit_dtz_hi()
            ckpt("bc_dtz")
            if dump and c == dumpc:
                P.wt("sync", DVE, DVE.n)
                P.wt("sync", ACT, st["gate_ev"])
                P.em("sync", lambda e: e.dma_start(out=dumps["d_z"][:, :], in_=z_bf[:, :, :]), inc=ST, k=16)
                P.em("sync", lambda e: e.dma_start(out=dumps["d_gate"][:, :], in_=gate_bf[:, :, :]), inc=ST, k=16)
                P.em("sync", lambda e: e.dma_start(out=dumps["d_dt"][:, :], in_=dt_bf[:, :, :]), inc=ST, k=16)
                P.wt("sync", ACT, st["bc_ev"])
                P.em("sync", lambda e: e.dma_start(out=dumps["d_Bt"][:, :], in_=btb[:, :]), inc=ST, k=16)
                P.em("sync", lambda e: e.dma_start(out=dumps["d_Ct"][:, :], in_=ctb[:, :]), inc=ST, k=16)
                dump_barrier()

            scan_base = SCAN.n
            if c == 0:
                P.wt("vector", DVE, hcol_ms)
                P.wt("scalar", DVE, aneg_ms)
            scan_ms = {}
            red_ms = {}
            u_ms_i = {}
            da_ms_i = {}
            last_yacc = 0
            pend_hv = None
            for sgrp in range(4):
                if sgrp > 0:
                    rep_last, crep_last = rep_build(sgrp)
                for db in range(16):
                    it = sgrp * 16 + db
                    if it == 5:
                        emit_dtz_hi()
                    pump(it)
                    if pend_hv is not None:
                        hv = P.em("vector", lambda e, a=pend_hv: e.tensor_copy(
                            out=hcol[:, a[1]:a[1] + 4], in_=H4[:, a[0], :, TC + 1]),
                            inc=DVE)
                        pend_hv = None
                    P.wt("scalar", DVE, z0)
                    P.wt("scalar", DVE, dt_clip[db])
                    da_h = [0, 0]
                    for h in range(2):
                        P.wt("scalar", SCAN, scan_base + 2 * (it - 1) + h + 1)
                        for j in range(2):
                            s_ = sgrp * 4 + 2 * h + j
                            v = P.em("scalar", lambda e, db=db, s_=s_, h=h, j=j:
                                     e.activation(
                                out=dA4[:, h, j, 2:TC + 2], in_=dt_bf[:, db, :],
                                func=AF.Exp,
                                scale=aneg[:, db * DS + s_: db * DS + s_ + 1]),
                                inc=ACT)
                        da_h[h] = v
                    if db == 0 and sgrp == 0:
                        P.wt("vector", ACT, rep_last)
                        P.wt("vector", DVE, dtz_ms)
                    P.wt("vector", ACT, rep_last)
                    cidx = db * 16 + sgrp * 4
                    P.em("vector", lambda e, cidx=cidx: e.tensor_copy(
                        out=U4[:, :, 1], in_=hcol[:, cidx:cidx + 4]))
                    pit = it - 1
                    pslot = pit % 2
                    psg, pdb = pit // 16, pit % 16
                    slot = it % 2
                    sc = None
                    sc_h = [None, None]
                    rv = 0
                    uv = None
                    for j4 in (0, 2, 1, 3):
                        uv = P.em("vector", lambda e, db=db, j4=j4:
                                  e.tensor_tensor(
                            out=U4[:, j4, 2:TC + 2], in0=z_bf[:, db, :],
                            in1=brep4[:, j4, :], op=ALU.mult), inc=DVE)
                    u_ms_i[(sgrp, db)] = uv
                    for h in range(2):
                        P.wt("vector", DVE, uv)
                        P.wt("vector", ACT, da_h[h])
                        if it >= 2:
                            P.wt("vector", POOL, pool_ms.get(64 * c + it - 2, 0))
                        sc = P.em("vector", lambda e, slot=slot, h=h:
                                  e.tensor_tensor_scan(
                            out=H4[:, slot, 2 * h:2 * h + 2, :].rearrange(
                                "p s t -> p (s t)"),
                            data0=dA4[:, h, :, :].rearrange("p s t -> p (s t)"),
                            data1=U4[:, 2 * h:2 * h + 2, :].rearrange(
                                "p s t -> p (s t)"),
                            initial=0.0, op0=ALU.mult, op1=ALU.add), inc=SCAN)
                        sc_h[h] = sc
                        if h == 0 and it >= 1:
                            if pit == 0:
                                P.wt("vector", PE, st["wo_pe"])
                            P.wt("vector", POOL, pool_ms[64 * c + pit])
                            rv = P.em("vector", lambda e: e.tensor_tensor(
                                out=t1r[:, :, :], in0=t1r[:, :, :],
                                in1=t2r[:, :, :], op=ALU.add), inc=RED)
                            red_ms[pit] = rv
                            if psg == 0:
                                last_yacc = P.em("vector", lambda e, pdb=pdb:
                                                 e.tensor_copy(
                                    out=yg_bf[:, pdb, :], in_=t1r[:, 0, :]),
                                    inc=DVE)
                            else:
                                last_yacc = P.em("vector", lambda e, pdb=pdb:
                                                 e.tensor_tensor(
                                    out=yg_bf[:, pdb, :], in0=yg_bf[:, pdb, :],
                                    in1=t1r[:, 0, :], op=ALU.add), inc=DVE)
                            last_yacc = P.em("vector", lambda e, pdb=pdb:
                                             e.tensor_tensor(
                                out=yg_bf[:, pdb, :], in0=yg_bf[:, pdb, :],
                                in1=t1r[:, 1, :], op=ALU.add), inc=DVE)
                    scan_ms[it] = sc
                    pend_hv = (slot, cidx)
                    P.wt("gpsimd", SCAN, sc_h[0])
                    if it >= 1:
                        P.wt("gpsimd", RED, red_ms[pit])
                        P.wt("gpsimd", DVE, last_yacc)
                    else:
                        P.wt("gpsimd", DVE, st["ygfin"])
                    if db == 0:
                        P.wt("gpsimd", ACT, crep_last)
                    P.em("gpsimd", lambda e, slot=slot: e.tensor_tensor(
                        out=t1r[:, :, :], in0=H4[:, slot, 0:2, 2:TC + 2],
                        in1=crep4[:, 0:2, :], op=ALU.mult), inc=POOL)
                    P.wt("gpsimd", SCAN, sc_h[1])
                    pm_ = P.em("gpsimd", lambda e, slot=slot: e.tensor_tensor(
                        out=t2r[:, :, :], in0=H4[:, slot, 2:4, 2:TC + 2],
                        in1=crep4[:, 2:4, :], op=ALU.mult), inc=POOL)
                    pool_ms[64 * c + it] = pm_
            st["u_last"] = u_ms_i[(3, 15)]
            if pend_hv is not None:
                hv = P.em("vector", lambda e, a=pend_hv: e.tensor_copy(
                    out=hcol[:, a[1]:a[1] + 4], in_=H4[:, a[0], :, TC + 1]),
                    inc=DVE)
                pend_hv = None
            it = 63
            P.wt("vector", POOL, pool_ms[64 * c + it])
            rv = P.em("vector", lambda e: e.tensor_tensor(
                out=t1r[:, :, :], in0=t1r[:, :, :], in1=t2r[:, :, :],
                op=ALU.add), inc=RED)
            red_ms[it] = rv
            P.wt("vector", RED, rv)
            last_yacc = P.em("vector", lambda e: e.tensor_tensor(
                out=yg_bf[:, 15, :], in0=yg_bf[:, 15, :], in1=t1r[:, 0, :],
                op=ALU.add), inc=DVE)
            last_yacc = P.em("vector", lambda e: e.tensor_tensor(
                out=yg_bf[:, 15, :], in0=yg_bf[:, 15, :], in1=t1r[:, 1, :],
                op=ALU.add), inc=DVE)
            P.wt("vector", DVE, last_yacc)
            P.wt("vector", ACT, st["gate_ev"])
            ygf = 0
            for db in range(16):
                ygf = P.em("vector", lambda e, db=db: e.tensor_tensor(
                    out=yg_bf[:, db, :], in0=yg_bf[:, db, :], in1=gate_bf[:, db, :],
                    op=ALU.mult), inc=YG)
            st["ygfin"] = P.em("vector", lambda e: e.tensor_copy(
                out=sml[:, 87:88], in_=sml[:, 87:88]), inc=DVE)
            if dump and c == dumpc:
                P.wt("sync", YG, 16 * (c + 1))
                P.em("sync", lambda e: e.dma_start(out=dumps["d_yg"][:, :], in_=yg_bf[:, :, :]), inc=ST, k=16)
                dump_barrier()
            ckpt("scan")
            drain()
            ckpt("drain")

    except _Trunc:
        pass

    es1p.close()
    es1.close()

    P.wt("sync", ST, ST.n)
    with nc.Block() as blk:
        for eng, meth in [("sync", blk.sync), ("scalar", blk.scalar),
                          ("vector", blk.vector), ("tensor", blk.tensor),
                          ("gpsimd", blk.gpsimd)]:
            def runner(e, eng=eng):
                for thn in P.p[eng]:
                    thn(e)
            meth(runner)
    es.close()
    return nc, dumps


def host_inputs(inp, b):
    g = lambda k: np.ascontiguousarray(inp[k])
    return {
        "norm_w": g("L0_attn_norm_w").reshape(1, D),
        "norm_b": g("L0_attn_norm_b").reshape(1, D),
        "W_in": g("L0_attn_W_in"),
        "W_gate": g("L0_attn_W_gate"),
        "W_dt": g("L0_attn_W_dt"),
        "b_dt": g("L0_attn_b_dt").reshape(DI // 128, 128),
        "W_B": g("L0_attn_W_B"),
        "W_C": g("L0_attn_W_C"),
        "A_log": g("L0_attn_A_log"),
        "W_out": g("L0_attn_W_out"),
        "router_w": g("L0_ffn_router_w"),
        "w1": g("L0_ffn_w1").reshape(NE * D, D),
        "w2": g("L0_ffn_w2").reshape(NE * D, D),
        "sgu": g("L0_ffn_shared_gate_up"),
        "sdown": g("L0_ffn_shared_down"),
        "sgw": g("L0_ffn_shared_gate_w").reshape(D // 128, 128),
    }


def kernel(**inputs):
    from concourse.bass_utils import run_bass_kernel_spmd
    X = np.asarray(inputs["L0_attn_x"])
    nc, _ = build(T=4096, TC=512, dump=False)
    base = host_inputs(inputs, 0)
    in_maps = []
    for b in range(8):
        m = dict(base)
        m["x"] = np.ascontiguousarray(X[b])
        in_maps.append(m)
    res = run_bass_kernel_spmd(nc, in_maps, core_ids=list(range(8)))
    return np.stack([r["out"] for r in res.results], axis=0)


# version 17 (final)
# speedup vs baseline: 1.0262x; 1.0262x over previous
import numpy as np
import ml_dtypes
import concourse.bass as bass
import concourse.mybir as mybir
from contextlib import ExitStack

F32 = mybir.dt.float32
BF16 = mybir.dt.bfloat16
AF = mybir.ActivationFunctionType
ALU = mybir.AluOpType
AX = mybir.AxisListType

D = 1024
DI = 2048
DS = 16
NE = 8
EPS = 1e-5


class Sem:
    def __init__(self, es, nc, name):
        self.h = es.enter_context(nc.semaphore(name))
        self.n = 0
        self.name = name


class Prog:
    ENGS = ("sync", "scalar", "vector", "tensor", "gpsimd")

    def __init__(self):
        self.p = {k: [] for k in self.ENGS}
        self.last_wait = {}

    def em(self, eng, fn, inc=None, k=1):
        v = None
        if inc is not None:
            inc.n += k
            v = inc.n

        def th(e, fn=fn, inc=inc, k=k):
            ins = fn(e)
            if inc is not None:
                ins.then_inc(inc.h, k)
        self.p[eng].append(th)
        return v

    def wt(self, eng, sem, v):
        if v is None or v <= 0:
            return
        key = (eng, sem.name)
        if self.last_wait.get(key, 0) >= v:
            return
        self.last_wait[key] = v
        self.p[eng].append(lambda e, s=sem, v=v: e.wait_ge(s.h, v))


class _Trunc(Exception):
    pass


def build(T=4096, TC=512, dump=False, trunc=None, dumpc=0, sync_stages=(),
          sched=None):
    nch = T // TC
    nti = TC // 128
    nc = bass.Bass("TRN2", target_bir_lowering=False)

    x_d = nc.dram_tensor("x", [T, D], F32, kind="ExternalInput").ap()
    nw_d = nc.dram_tensor("norm_w", [1, D], F32, kind="ExternalInput").ap()
    nb_d = nc.dram_tensor("norm_b", [1, D], F32, kind="ExternalInput").ap()
    win_d = nc.dram_tensor("W_in", [D, DI], F32, kind="ExternalInput").ap()
    wgate_d = nc.dram_tensor("W_gate", [D, DI], F32, kind="ExternalInput").ap()
    wdt_d = nc.dram_tensor("W_dt", [DI, DI], F32, kind="ExternalInput").ap()
    bdt_d = nc.dram_tensor("b_dt", [DI // 128, 128], F32, kind="ExternalInput").ap()
    wb_d = nc.dram_tensor("W_B", [DI, DS], F32, kind="ExternalInput").ap()
    wc_d = nc.dram_tensor("W_C", [DI, DS], F32, kind="ExternalInput").ap()
    alog_d = nc.dram_tensor("A_log", [DI, DS], F32, kind="ExternalInput").ap()
    wout_d = nc.dram_tensor("W_out", [DI, D], F32, kind="ExternalInput").ap()
    rw_d = nc.dram_tensor("router_w", [D, NE], F32, kind="ExternalInput").ap()
    w1_d = nc.dram_tensor("w1", [NE * D, D], F32, kind="ExternalInput").ap()
    w2_d = nc.dram_tensor("w2", [NE * D, D], F32, kind="ExternalInput").ap()
    sgu_d = nc.dram_tensor("sgu", [D, 2 * D], F32, kind="ExternalInput").ap()
    sdn_d = nc.dram_tensor("sdown", [D, D], F32, kind="ExternalInput").ap()
    sgw_d = nc.dram_tensor("sgw", [D // 128, 128], F32, kind="ExternalInput").ap()
    out_d = nc.dram_tensor("out", [T, D], F32, kind="ExternalOutput").ap()

    ws_in = nc.dram_tensor("ws_in", [128, 2 * 8192], BF16, kind="Internal").ap()
    ws_gate = nc.dram_tensor("ws_gate", [128, 2 * 8192], BF16, kind="Internal").ap()
    ws_dt = nc.dram_tensor("ws_dt", [128, 4 * 8192], BF16, kind="Internal").ap()
    ws_out = nc.dram_tensor("ws_out", [128, 2 * 8192], BF16, kind="Internal").ap()
    ws_w1 = nc.dram_tensor("ws_w1", [128, 8 * 8192], BF16, kind="Internal").ap()
    ws_w2 = nc.dram_tensor("ws_w2", [128, 8 * 8192], BF16, kind="Internal").ap()
    ws_sgu = nc.dram_tensor("ws_sgu", [128, 2 * 8192], BF16, kind="Internal").ap()
    ws_sdn = nc.dram_tensor("ws_sdn", [128, 8192], BF16, kind="Internal").ap()

    ident_bf_d = nc.inline_tensor(np.eye(128, dtype=ml_dtypes.bfloat16), "identbf").ap()
    oh16_np = np.zeros((16, 16 * 128), dtype=ml_dtypes.bfloat16)
    for s_ in range(16):
        oh16_np[s_, s_ * 128:(s_ + 1) * 128] = 1
    oh16_d = nc.inline_tensor(oh16_np, "oh16c").ap()
    ones1_d = nc.inline_tensor(np.ones((1, 128), dtype=np.float32), "ones1c").ap()

    dumps = {}
    if dump:
        for nm, shape, dt in [
            ("d_z", [128, 16 * TC], BF16), ("d_gate", [128, 16 * TC], BF16),
            ("d_dt", [128, 16 * TC], BF16), ("d_Bt", [16, TC], BF16),
            ("d_Ct", [16, TC], BF16), ("d_yg", [128, 16 * TC], BF16),
            ("d_mo", [128, 8 * TC], BF16), ("d_xf", [128, 8 * TC], BF16),
            ("d_comb", [128, (TC // 128) * 8], F32), ("d_xn", [128, 8 * TC], BF16),
            ("d_shx", [128, 8 * TC], BF16), ("d_ffn", [128, 8 * TC], BF16),
        ]:
            dumps[nm] = nc.dram_tensor(nm, shape, dt, kind="ExternalOutput").ap()

    P = Prog()
    es = ExitStack()
    S = {k: Sem(es, nc, k) for k in (
        "STG", "LD", "W", "X", "X2", "ST", "PE", "ACT", "DVE",
        "SCAN", "POOL", "RED", "YG")}
    STG, LD, W, X, X2, ST, PE, ACT, DVE, SCAN, POOL, RED, YG = (
        S[k] for k in ("STG", "LD", "W", "X", "X2", "ST", "PE", "ACT", "DVE",
                       "SCAN", "POOL", "RED", "YG"))

    def sb(name, shape, dt, stack=es):
        return stack.enter_context(nc.sbuf_tensor(name, shape, dt))

    def psum(name, shape, dt, stack):
        return stack.enter_context(nc.psum_tensor(name, shape, dt))

    _ck = [0]

    def ckpt(nm):
        _ck[0] += 1
        if trunc is not None and _ck[0] >= trunc:
            print(f"TRUNC at ckpt {_ck[0]}: {nm}", flush=True)
            raise _Trunc
        if nm in sync_stages or "all" in sync_stages:
            for eng in ("sync", "scalar", "vector", "tensor", "gpsimd"):
                for sem_ in (DVE, ACT, PE, SCAN, RED, POOL, YG, X2, ST, W, X,
                             STG, LD):
                    P.wt(eng, sem_, sem_.n)

    ident_bf = sb("ident_bf", [128, 128], BF16)
    oh16 = sb("oh16", [16, 16 * 128], BF16)
    ones1b = oh16[0:1, 0:128]
    oh8 = oh16[0:8, 0:8 * 128]
    nw_rep = sb("nw_rep", [128, D], BF16)
    nb_rep = sb("nb_rep", [128, D], BF16)
    aneg = sb("aneg", [128, 16 * DS], F32)
    bdt_sb = sb("bdt_sb", [128, 16], F32)
    wbb = sb("wbb", [128, 16 * DS], BF16)
    wcb = sb("wcb", [128, 16 * DS], BF16)
    rwb = sb("rwb", [128, 8 * NE], BF16)
    sgwb = sb("sgwb", [128, 8], BF16)
    hcol = sb("hcol", [128, 256], BF16)
    cvals = sb("cvals", [128, 8], F32)
    es_p0x = ExitStack()
    ones1 = sb("ones1", [1, 128], F32, es_p0x)
    nrow = sb("nrow", [1, D], F32, es_p0x)
    atmp = sb("atmp", [128, 16 * DS], F32, es_p0x)

    def dump_barrier():
        for _eng in ("scalar", "vector", "tensor", "gpsimd"):
            P.wt(_eng, ST, ST.n)

    for _i, _v in enumerate([EPS, 1.0 / D, -0.5, -1.0, 1.0, 0.001]):
        nc.const_aps.register(F32, _v, cvals[:, _i:_i + 1])
        P.em("vector", lambda e, _i=_i, _v=_v: e.memset(cvals[:, _i:_i + 1], _v))
    stage_ms = {}
    for nm, dst, srcd, nsub, ncols in [
        ("in", ws_in, win_d, 2, DI), ("gate", ws_gate, wgate_d, 2, DI),
        ("dt", ws_dt, wdt_d, 4, DI), ("out", ws_out, wout_d, 2, D),
        ("w1", ws_w1, w1_d, 8, D), ("w2", ws_w2, w2_d, 8, D),
        ("sgu", ws_sgu, sgu_d, 2, DI), ("sdn", ws_sdn, sdn_d, 1, D),
    ]:
        nk = 8192 // ncols
        for si in range(nsub):
            stage_ms[nm] = P.em("gpsimd", lambda e, d=dst, s=srcd, si=si, nk=nk,
                                ncols=ncols: e.dma_start(
                out=d[:, si * 8192:(si + 1) * 8192].rearrange(
                    "p (kb n) -> p kb n", n=ncols),
                in_=s[si * nk * 128:(si + 1) * nk * 128, :].rearrange(
                    "(kb p) n -> p kb n", p=128)), inc=STG, k=16)
    wb_ld = P.em("gpsimd", lambda e: e.dma_start(
        out=wbb[:].rearrange("p (kb s) -> p kb s", s=DS),
        in_=wb_d.rearrange("(kb p) s -> p kb s", p=128)), inc=STG, k=16)
    wc_ld = P.em("gpsimd", lambda e: e.dma_start(
        out=wcb[:].rearrange("p (kb s) -> p kb s", s=DS),
        in_=wc_d.rearrange("(kb p) s -> p kb s", p=128)), inc=STG, k=16)
    rw_ld = P.em("gpsimd", lambda e: e.dma_start(
        out=rwb[:].rearrange("p (kb s) -> p kb s", s=NE),
        in_=rw_d.rearrange("(kb p) s -> p kb s", p=128)), inc=STG, k=16)
    sgw_ld = None
    for kb in range(8):
        sgw_ld = P.em("gpsimd", lambda e, kb=kb: e.dma_start(
            out=sgwb[:, kb:kb + 1],
            in_=sgw_d.rearrange("a p -> p a")[:, kb:kb + 1]), inc=STG, k=16)
    for dst, src in [(ident_bf, ident_bf_d),
                     (oh16, oh16_d), (ones1, ones1_d)]:
        P.em("sync", lambda e, d=dst, s=src: e.dma_start(out=d[:], in_=s[:, :]),
             inc=LD, k=16)
    nwld = P.em("sync", lambda e: e.dma_start(out=nrow[:, :], in_=nw_d[:, :]), inc=LD, k=16)
    P.em("sync", lambda e: e.dma_start(
        out=atmp[:].rearrange("p (db s) -> p db s", s=DS),
        in_=alog_d.rearrange("(db p) s -> p db s", p=128)), inc=LD, k=16)
    for kb in range(16):
        P.em("sync", lambda e, kb=kb: e.dma_start(
            out=bdt_sb[:, kb:kb + 1],
            in_=bdt_d.rearrange("a p -> p a")[:, kb:kb + 1]), inc=LD, k=16)
    consts_ld = LD.n

    es_p0 = ExitStack()
    ps0_t = psum("ps0t", [128, 512], F32, es_p0)
    P.wt("tensor", LD, nwld)
    nrep_last = 0
    for pass_, dst in [(0, nw_rep), (1, nb_rep)]:
        if pass_ == 1:
            P.wt("sync", ACT, nrep_last)
            nbld = P.em("sync", lambda e: e.dma_start(out=nrow[:, :], in_=nb_d[:, :]),
                        inc=LD, k=16)
            P.wt("tensor", LD, nbld)
        for j in range(D // 512):
            m = P.em("tensor", lambda e, j=j: e.matmul(
                ps0_t[:, :], ones1[:, :], nrow[:, j * 512:(j + 1) * 512],
                start=True, stop=True), inc=PE)
            P.wt("scalar", PE, m)
            ev = P.em("scalar", lambda e, d=dst, j=j: e.activation(
                out=d[:, j * 512:(j + 1) * 512], in_=ps0_t[:, :], func=AF.Copy), inc=ACT)
            P.wt("tensor", ACT, ev)
            nrep_last = ev
    es_p0.close()
    P.wt("scalar", LD, consts_ld)
    a_e = P.em("scalar", lambda e: e.activation(out=atmp[:], in_=atmp[:], func=AF.Exp),
               inc=ACT)
    P.wt("vector", ACT, a_e)
    vc = P.em("vector", lambda e: e.tensor_scalar(
        out=atmp[:], in0=atmp[:], scalar1=1.0, scalar2=1e-6, op0=ALU.min,
        op1=ALU.max), inc=DVE)
    P.wt("vector", DVE, vc)
    aneg_ms = P.em("vector", lambda e: e.tensor_scalar(
        out=aneg[:], in0=atmp[:], scalar1=-1.0, scalar2=None, op0=ALU.mult), inc=DVE)
    hcol_ms = P.em("vector", lambda e: e.memset(hcol[:], 0.0), inc=DVE)
    p0_act = ACT.n
    p0_dve = DVE.n
    es_p0x.close()

    es1 = ExitStack()
    x_tm = sb("x_tm", [128, nti, D], BF16, es1)
    sq_b = sb("sq_b", [128, D], BF16, es1)
    xt = sb("xt", [128, D], F32, es1)
    xn_tm = sb("xn_tm", [128, D], BF16, es1)
    z_bf = sb("z_bf", [128, 16, TC], BF16, es1)
    gate_bf = sb("gate_bf", [128, 16, TC], BF16, es1)
    dt_bf = sb("dt_bf", [128, 16, TC], BF16, es1)
    yg_bf = sb("yg_bf", [128, 16, TC], BF16, es1)
    btb = sb("btb", [16, TC], BF16, es1)
    ctb = sb("ctb", [16, TC], BF16, es1)
    bcrep = sb("bcrep", [128, 8, TC], BF16, es1)
    brepA = bcrep[:, 0:4, :]
    crep4 = bcrep[:, 4:8, :]
    dA4 = sb("dA4", [128, 2, 2, TC + 2], F32, es1)
    U4 = sb("U4", [128, 4, TC + 2], BF16, es1)
    H4 = sb("H4", [128, 2, 4, TC + 2], BF16, es1)
    t1r = sb("t1r", [128, 2, TC], BF16, es1)
    t2r = sb("t2r", [128, 2, TC], BF16, es1)
    sml = sb("sml", [128, 88], F32, es1)
    RW = sb("RW", [128, 3, 8192], BF16, es1)
    xf = sb("xf", [128, 8, TC], BF16, es1)
    scaled = sb("scaled", [128, 8, TC], BF16, es1)
    rep2 = sb("rep2", [128, 1, TC], BF16, es1)
    ffn_acc = sb("ffn_acc", [128, 8, TC], BF16, es1)
    th2 = sb("th2", [128, 2, TC], BF16, es1)
    etmp = sb("etmp", [128, 2, TC], BF16, es1)
    out_tm = sb("out_tm", [128, D], F32, es1)
    comb_fm = sb("comb_fm", [8, TC], BF16, es1)
    comb_tm = sb("comb_tm", [128, 4 * 8], F32, es1)
    comb_bf = sb("comb_bf", [128, 4 * 8], BF16, es1)
    lgq = sb("lgq", [128, 8], F32, es1)
    prb = sb("prb", [128, 8], F32, es1)
    is1 = sb("is1", [128, 8], F32, es1)

    mo_fm = scaled
    shx = scaled
    sgv = btb[0:1, :]
    xn_fm = dA4[:].rearrange("p h s t -> p (h s t)").bitcast(BF16)[
        :, 0:8 * TC].rearrange("p (kb n) -> p kb n", n=TC)
    tmp_e = sq_b[:].bitcast(F32)
    xt_bf = xt[:].bitcast(BF16)
    x2slot = [xt_bf[:, 0:D], xt_bf[:, D:2 * D]]
    brepB = xt_bf[:].rearrange("p (a b) -> p a b", a=4)
    tmp_l3 = xt[:].rearrange("p (sl n) -> p sl n", sl=2)
    x3_tm = th2[:].rearrange("p a b -> p (a b)")

    es1p = ExitStack()
    ps = [psum(f"ps{i}", [128, 512], F32, es1p) for i in range(8)]
    psT = [ps[6][:, 0:64].bitcast(BF16), ps[7][:, 0:64].bitcast(BF16)]

    bev = {i: (ACT, 0) for i in range(8)}

    def use_bank(i):
        sem, v = bev[i]
        P.wt("tensor", sem, v)

    def set_bank(i, sem, v):
        bev[i] = (sem, v)

    ring_cons = [0, 0, 0]
    ring_i = [0]

    def wreq(src_fn, stg):
        i = ring_i[0] % 3
        ring_i[0] += 1
        P.wt("sync", PE, ring_cons[i])
        P.wt("sync", STG, stage_ms[stg])
        wv = P.em("sync", lambda e, i=i, src_fn=src_fn: e.dma_start(
            out=RW[:, i, :], in_=src_fn()), inc=W, k=16)
        return [i, wv]

    def wuse(h):
        P.wt("tensor", W, h[1])

    def wview(h, ncols):
        return RW[:, h[0], :].rearrange("p (kb n) -> p kb n", n=ncols)

    def wdone(h, pe_ms):
        ring_cons[h[0]] = pe_ms

    sub = lambda t, i: (lambda: t[:, i * 8192:(i + 1) * 8192])
    win_src = lambda hh: sub(ws_in, hh)
    gate_src = lambda hh: sub(ws_gate, hh)
    dt_src = lambda q: sub(ws_dt, q)
    wout_src = lambda hh: sub(ws_out, hh)
    w1_src = lambda e_: sub(ws_w1, e_)
    w2_src = lambda e_: sub(ws_w2, e_)
    sgu_src = lambda hh: sub(ws_sgu, hh)
    sdn_src = sub(ws_sdn, 0)

    def ln_group(src_fns, cols_list):
        n = len(src_fns)
        r1s, sqs, v2s, v1s, v2ms, v3s, a2s, v4s = ([None] * n for _ in range(8))
        for i in range(n):
            r1s[i] = P.em("vector", lambda e, i=i: e.reduce_sum(
                out=sml[:, cols_list[i]:cols_list[i] + 1], in_=src_fns[i](),
                axis=AX.X), inc=DVE)
        for i in range(n):
            if i > 0:
                P.wt("scalar", DVE, v2s[i - 1])
            sqs[i] = P.em("scalar", lambda e, i=i: e.activation(
                out=sq_b[:], in_=src_fns[i](), func=AF.Square), inc=ACT)
            P.wt("vector", ACT, sqs[i])
            v2s[i] = P.em("vector", lambda e, i=i: e.reduce_sum(
                out=sml[:, cols_list[i] + 1:cols_list[i] + 2], in_=sq_b[:],
                axis=AX.X), inc=DVE)
        for i in range(n):
            P.wt("vector", DVE, r1s[i])
            v1s[i] = P.em("vector", lambda e, i=i: e.tensor_scalar(
                out=sml[:, cols_list[i] + 2:cols_list[i] + 3],
                in0=sml[:, cols_list[i]:cols_list[i] + 1],
                scalar1=-1.0 / D, scalar2=None, op0=ALU.mult), inc=DVE)
        for i in range(n):
            P.wt("vector", DVE, v1s[i])
            v2ms[i] = P.em("vector", lambda e, i=i: e.scalar_tensor_tensor(
                out=sml[:, cols_list[i] + 3:cols_list[i] + 4],
                in0=sml[:, cols_list[i] + 2:cols_list[i] + 3],
                scalar=-1.0, op0=ALU.mult,
                in1=sml[:, cols_list[i] + 2:cols_list[i] + 3],
                op1=ALU.mult), inc=DVE)
        for i in range(n):
            P.wt("vector", DVE, v2ms[i])
            P.wt("vector", DVE, v2s[i])
            v3s[i] = P.em("vector", lambda e, i=i: e.scalar_tensor_tensor(
                out=sml[:, cols_list[i] + 4:cols_list[i] + 5],
                in0=sml[:, cols_list[i] + 1:cols_list[i] + 2],
                scalar=1.0 / D, op0=ALU.mult,
                in1=sml[:, cols_list[i] + 3:cols_list[i] + 4],
                op1=ALU.add), inc=DVE)
        for i in range(n):
            P.wt("scalar", DVE, v3s[i])
            aL = P.em("scalar", lambda e, i=i: e.activation(
                out=sml[:, cols_list[i] + 5:cols_list[i] + 6],
                in_=sml[:, cols_list[i] + 4:cols_list[i] + 5],
                func=AF.Ln, bias=EPS), inc=ACT)
            P.wt("scalar", ACT, aL)
            a2s[i] = P.em("scalar", lambda e, i=i: e.activation(
                out=sml[:, cols_list[i] + 6:cols_list[i] + 7],
                in_=sml[:, cols_list[i] + 5:cols_list[i] + 6],
                func=AF.Exp, scale=-0.5), inc=ACT)
        for i in range(n):
            P.wt("vector", ACT, a2s[i])
            v4s[i] = P.em("vector", lambda e, i=i: e.tensor_tensor(
                out=sml[:, cols_list[i] + 7:cols_list[i] + 8],
                in0=sml[:, cols_list[i] + 2:cols_list[i] + 3],
                in1=sml[:, cols_list[i] + 6:cols_list[i] + 7],
                op=ALU.mult), inc=DVE)
        return a2s, v4s

    def ln_core(src_fn, cols, out_fn):
        a2s, v4s = ln_group([src_fn], [cols])
        P.wt("scalar", DVE, v4s[0])
        P.wt("scalar", ACT, a2s[0])
        return out_fn(cols + 6, cols + 7), 0

    st = dict(u_last=0, ygfin=0, rep_pe=0, ln2_act=0, xn_dve=0, x2_dve=0, xn_tp=0, gate_ev=0,
              xf_tp=0, xf_act=0, sdn_pe=0, ln3_tp=0, resid_x2=0, bc_ev=0,
              dt_clip15=0, gate_pe=0, scmul_last=0, gu_x2=0, x3_stats_dve=0,
              mo_ev=[0] * 8, out_ev=0, wo_pe=0, ygfin_prev=0)
    xslot_pe = [0, 0]
    rep_cons = [0, 0]
    th_cons = [0, 0]
    etmp_cons = [0, 0]
    u_ms_i = {}
    pool_ms = {}

    def ffn_segments(kk):
        segs = []
        fst = dict(comb_done=0, cast_ms=0, rep_ev=[0, 0], silu=[0, 0],
                   scm=[0, 0], fev=[0, 0], w1h=None, w2h=None, sguh=[None, None],
                   sdnh=None, last_w2=0, sgv_act=0, sgrep_ev=0, shx_x2=0,
                   gu_pe=0, ffn_add=st["resid_x2"], scmul=[0] * 8)

        def dv(fn):
            vv = P.em("vector", fn, inc=DVE)
            P.wt("vector", DVE, vv)
            return vv

        def seg_router(g):
            P.wt("tensor", STG, rw_ld)
            P.wt("tensor", ACT, st["xf_act"])
            use_bank(0)
            if fst["comb_done"]:
                P.wt("tensor", DVE, fst["comb_done"])
            last_mmr = None
            for kb in range(8):
                last_mmr = P.em("tensor", lambda e, g=g, kb=kb: e.matmul(
                    ps[0][:, 0:8], xf[:, kb, g * 128:(g + 1) * 128],
                    rwb[:, kb * 8:(kb + 1) * 8], start=(kb == 0), stop=(kb == 7)),
                    inc=(PE if kb == 7 else None))
            P.wt("vector", PE, last_mmr)
            lg_ms = dv(lambda e: e.tensor_copy(out=lgq[:, :], in_=ps[0][:, 0:8]))
            set_bank(0, DVE, lg_ms)
            v = dv(lambda e: e.reduce_max(out=sml[:, 72:73], in_=lgq[:, :], axis=AX.X))
            v = dv(lambda e: e.tensor_scalar(
                out=sml[:, 72:73], in0=sml[:, 72:73], scalar1=-1.0, scalar2=None,
                op0=ALU.mult))
            P.wt("scalar", DVE, v)
            aq = P.em("scalar", lambda e: e.activation(
                out=prb[:, :], in_=lgq[:, :], func=AF.Exp, bias=sml[:, 72:73]),
                inc=ACT)
            P.wt("vector", ACT, aq)
            v = dv(lambda e: e.reduce_sum(out=sml[:, 73:74], in_=prb[:, :], axis=AX.X))
            P.wt("scalar", DVE, v)
            aL2 = P.em("scalar", lambda e: e.activation(
                out=sml[:, 74:75], in_=sml[:, 73:74], func=AF.Ln), inc=ACT)
            P.wt("scalar", ACT, aL2)
            aq = P.em("scalar", lambda e: e.activation(
                out=sml[:, 75:76], in_=sml[:, 74:75], func=AF.Exp, scale=-1.0),
                inc=ACT)
            P.wt("vector", ACT, aq)
            dv(lambda e: e.tensor_scalar(
                out=prb[:, :], in0=prb[:, :], scalar1=sml[:, 75:76], scalar2=None,
                op0=ALU.mult))
            dv(lambda e: e.reduce_max(out=sml[:, 76:77], in_=prb[:, :], axis=AX.X))
            dv(lambda e: e.tensor_scalar(
                out=is1[:, :], in0=prb[:, :], scalar1=sml[:, 76:77], scalar2=None,
                op0=ALU.is_ge))
            dv(lambda e: e.reduce_sum(out=sml[:, 77:78], in_=is1[:, :], axis=AX.X))
            dv(lambda e: e.scalar_tensor_tensor(
                out=is1[:, :], in0=is1[:, :], scalar=-2.0, op0=ALU.mult,
                in1=prb[:, :], op1=ALU.add))
            dv(lambda e: e.reduce_max(out=sml[:, 78:79], in_=is1[:, :], axis=AX.X))
            dv(lambda e: e.tensor_scalar(
                out=sml[:, 77:78], in0=sml[:, 77:78], scalar1=1.5, scalar2=None,
                op0=ALU.is_ge))
            dv(lambda e: e.tensor_tensor(
                out=sml[:, 79:80], in0=sml[:, 76:77], in1=sml[:, 78:79],
                op=ALU.subtract))
            dv(lambda e: e.tensor_tensor(
                out=sml[:, 79:80], in0=sml[:, 77:78], in1=sml[:, 79:80],
                op=ALU.mult))
            dv(lambda e: e.tensor_tensor(
                out=sml[:, 79:80], in0=sml[:, 78:79], in1=sml[:, 79:80],
                op=ALU.add))
            dv(lambda e: e.tensor_scalar(
                out=is1[:, :], in0=prb[:, :], scalar1=sml[:, 79:80], scalar2=None,
                op0=ALU.is_ge))
            fst["comb_done"] = dv(lambda e, g=g: e.tensor_tensor(
                out=comb_tm[:, g * 8:(g + 1) * 8], in0=prb[:, :], in1=is1[:, :],
                op=ALU.mult))

        def seg_comb():
            fst["cast_ms"] = P.em("vector", lambda e: e.tensor_copy(
                out=comb_bf[:, :], in_=comb_tm[:, :]), inc=DVE)
            if dump and kk == dumpc:
                P.wt("sync", DVE, fst["cast_ms"])
                P.em("sync", lambda e: e.dma_start(
                    out=dumps["d_comb"][:, :], in_=comb_tm[:, :]), inc=ST, k=16)
                dump_barrier()
            prev = 0
            for g in range(nti):
                sl = g % 2
                P.wt("tensor", DVE, fst["cast_ms"])
                use_bank(6 + sl)
                if prev:
                    P.wt("tensor", DVE, prev)
                tp = P.em("tensor", lambda e, g=g, sl=sl: e.transpose(
                    psT[sl][0:8, :], comb_bf[:, g * 8:(g + 1) * 8], ident_bf[:]),
                    inc=PE)
                P.wt("vector", PE, tp)
                prev = P.em("vector", lambda e, g=g, sl=sl: e.tensor_copy(
                    out=comb_fm[:, g * 128:(g + 1) * 128], in_=psT[sl][0:8, :]),
                    inc=DVE)
                set_bank(6 + sl, DVE, prev)
            fst["comb_fm_done"] = prev

        def seg_w1(e_, half):
            if e_ == 0 and half == 0:
                fst["w1h"] = wreq(w1_src(0), "w1")
                fst["w2h"] = wreq(w2_src(0), "w2")
            sl = e_ % 2
            if half == 0:
                use_bank(sl)
                P.wt("tensor", DVE, fst["comb_fm_done"])
                mm = P.em("tensor", lambda e, e_=e_, sl=sl: e.matmul(
                    ps[sl][:, :], oh8[:, e_ * 128:(e_ + 1) * 128], comb_fm[:, :],
                    start=True, stop=True), inc=PE)
                P.wt("scalar", PE, mm)
                P.wt("scalar", X2, rep_cons[0])
                rep_ev = P.em("scalar", lambda e, e_=e_, sl=sl: e.activation(
                    out=rep2[:, 0, :], in_=ps[sl][:, :], func=AF.Copy), inc=ACT)
                set_bank(sl, ACT, rep_ev)
                fst["rep_ev"][sl] = rep_ev
            w1h = fst["w1h"]
            wuse(w1h)
            w1v = wview(w1h, D)
            last_p = None
            for m in range(4 * half, 4 * half + 4):
                bsl = 4 + m % 2
                use_bank(bsl)
                for kb in range(8):
                    last_p = P.em("tensor", lambda e, m=m, kb=kb, w1v=w1v, bsl=bsl:
                                  e.matmul(ps[bsl][:, :],
                                           w1v[:, kb, m * 128:(m + 1) * 128],
                                           xf[:, kb, :], start=(kb == 0),
                                           stop=(kb == 7)),
                                  inc=(PE if kb == 7 else None))
                P.wt("scalar", PE, last_p)
                if e_ == 0 and m == 0:
                    P.wt("scalar", DVE, st["x2_dve"])
                    P.wt("scalar", PE, st["sdn_pe"])
                sil = P.em("scalar", lambda e, m=m, bsl=bsl: e.activation(
                    out=scaled[:, m, :], in_=ps[bsl][:, :], func=AF.Silu), inc=ACT)
                set_bank(bsl, ACT, sil)
                P.wt("gpsimd", ACT, sil)
                P.wt("gpsimd", ACT, fst["rep_ev"][sl])
                scm = P.em("gpsimd", lambda e, m=m:
                           e.tensor_tensor(out=scaled[:, m, :], in0=scaled[:, m, :],
                                           in1=rep2[:, 0, :], op=ALU.mult), inc=X2)
                fst["scmul"][m] = scm
            if half == 1:
                rep_cons[0] = fst["scmul"][7]
                fst["scm_last"] = fst["scmul"][7]
                wdone(w1h, last_p)
                fst["w1h"] = wreq(w1_src(e_ + 1), "w1") if e_ < NE - 1 else None

        def seg_w2(e_, half):
            w2h = fst["w2h"]
            wuse(w2h)
            w2v = wview(w2h, D)
            P.wt("tensor", X2, fst["scm_last"])
            last_w2 = None
            for m in range(4 * half, 4 * half + 4):
                fsl = 2 + m % 2
                use_bank(fsl)
                for kb in range(8):
                    last_w2 = P.em("tensor", lambda e, m=m, kb=kb, w2v=w2v, fsl=fsl:
                                   e.matmul(ps[fsl][:, :],
                                            w2v[:, kb, m * 128:(m + 1) * 128],
                                            scaled[:, kb, :], start=(kb == 0),
                                            stop=(kb == 7)),
                                   inc=(PE if kb == 7 else None))
                P.wt("scalar", PE, last_w2)
                if e_ == 0:
                    if m < 2:
                        P.wt("scalar", PE, st["ln3_tp"])
                        P.wt("scalar", X2, st["resid_x2"])
                    ev = P.em("scalar", lambda e, m=m, fsl=fsl: e.activation(
                        out=ffn_acc[:, m, :], in_=ps[fsl][:, :], func=AF.Copy),
                        inc=ACT)
                else:
                    esl = m % 2
                    P.wt("scalar", X2, etmp_cons[esl])
                    ev = P.em("scalar", lambda e, m=m, fsl=fsl, esl=esl: e.activation(
                        out=etmp[:, esl, :], in_=ps[fsl][:, :], func=AF.Copy),
                        inc=ACT)
                    P.wt("gpsimd", ACT, ev)
                    ad = P.em("gpsimd", lambda e, m=m, esl=esl: e.tensor_tensor(
                        out=ffn_acc[:, m, :], in0=ffn_acc[:, m, :],
                        in1=etmp[:, esl, :], op=ALU.add), inc=X2)
                    etmp_cons[esl] = ad
                    fst["ffn_add"] = ad
                set_bank(fsl, ACT, ev)
                fst["fev"][m % 2] = ev
            if half == 1:
                wdone(w2h, last_w2)
                fst["last_w2"] = last_w2
                if e_ < NE - 1:
                    fst["w2h"] = wreq(w2_src(e_ + 1), "w2")
                else:
                    fst["sguh"][0] = wreq(sgu_src(0), "sgu")
                    fst["sguh"][1] = wreq(sgu_src(1), "sgu")

        def seg_sgate():
            P.wt("tensor", STG, sgw_ld)
            use_bank(5)
            last_mmr = None
            for kb in range(8):
                last_mmr = P.em("tensor", lambda e, kb=kb: e.matmul(
                    ps[5][0:1, :], sgwb[:, kb:kb + 1], xf[:, kb, :],
                    start=(kb == 0), stop=(kb == 7)), inc=(PE if kb == 7 else None))
            P.wt("scalar", PE, last_mmr)
            P.wt("scalar", PE, st["rep_pe"])
            asg = P.em("scalar", lambda e: e.activation(
                out=sgv, in_=ps[5][0:1, :], func=AF.Sigmoid), inc=ACT)
            set_bank(5, ACT, asg)
            fst["sgv_act"] = asg
            P.wt("tensor", ACT, asg)
            use_bank(2)
            mm = P.em("tensor", lambda e: e.matmul(
                ps[2][:, :], ones1b[:, :], sgv, start=True, stop=True), inc=PE)
            P.wt("scalar", PE, mm)
            P.wt("scalar", X2, rep_cons[0])
            P.wt("scalar", X2, rep_cons[1])
            ev = P.em("scalar", lambda e: e.activation(
                out=rep2[:, 0, :], in_=ps[2][:, :], func=AF.Copy), inc=ACT)
            set_bank(2, ACT, ev)
            fst["sgrep_ev"] = ev

        def seg_gu(mg):
            sguv0 = wview(fst["sguh"][0], DI)
            sguv1 = wview(fst["sguh"][1], DI)
            wuse(fst["sguh"][0])
            wuse(fst["sguh"][1])
            use_bank(4)
            gumm = None
            for kb in range(8):
                v0 = sguv0 if kb < 4 else sguv1
                P.em("tensor", lambda e, mg=mg, kb=kb, v0=v0: e.matmul(
                    ps[4][:, :], v0[:, kb % 4, mg * 128:(mg + 1) * 128],
                    xf[:, kb, :], start=(kb == 0), stop=(kb == 7)),
                    inc=(PE if kb == 7 else None))
            use_bank(5)
            for kb in range(8):
                v0 = sguv0 if kb < 4 else sguv1
                gumm = P.em("tensor", lambda e, mg=mg, kb=kb, v0=v0: e.matmul(
                    ps[5][:, :], v0[:, kb % 4, (mg + 8) * 128:(mg + 9) * 128],
                    xf[:, kb, :], start=(kb == 0), stop=(kb == 7)),
                    inc=(PE if kb == 7 else None))
            P.wt("scalar", PE, gumm)
            P.wt("scalar", X2, th_cons[0])
            if mg == 0:
                P.wt("scalar", DVE, st["x3_stats_dve"])
            ag = P.em("scalar", lambda e: e.activation(
                out=th2[:, 0, :], in_=ps[4][:, :], func=AF.Silu), inc=ACT)
            set_bank(4, ACT, ag)
            P.wt("scalar", X2, th_cons[1])
            au = P.em("scalar", lambda e: e.activation(
                out=th2[:, 1, :], in_=ps[5][:, :], func=AF.Copy), inc=ACT)
            set_bank(5, ACT, au)
            P.wt("gpsimd", ACT, au)
            P.wt("gpsimd", ACT, fst["sgrep_ev"])
            if mg == 0:
                P.wt("gpsimd", PE, fst["last_w2"])
            m1 = P.em("gpsimd", lambda e: e.tensor_tensor(
                out=th2[:, 1, :], in0=th2[:, 0, :], in1=th2[:, 1, :],
                op=ALU.mult), inc=X2)
            th_cons[0] = m1
            m2 = P.em("gpsimd", lambda e, mg=mg: e.tensor_tensor(
                out=shx[:, mg, :], in0=th2[:, 1, :], in1=rep2[:, 0, :],
                op=ALU.mult), inc=X2)
            th_cons[1] = m2
            fst["shx_x2"] = m2
            if mg == NE - 1:
                rep_cons[0] = m2
                wdone(fst["sguh"][0], gumm)
                wdone(fst["sguh"][1], gumm)
                fst["gu_pe"] = gumm
                fst["sdnh"] = wreq(sdn_src, "sdn")
                if dump and kk == dumpc:
                    P.wt("sync", X2, m2)
                    P.em("sync", lambda e: e.dma_start(
                        out=dumps["d_shx"][:, :], in_=shx[:, :, :]), inc=ST, k=16)
                    dump_barrier()

        def seg_sdn():
            sdv = wview(fst["sdnh"], D)
            wuse(fst["sdnh"])
            P.wt("tensor", X2, fst["shx_x2"])
            sdmm = None
            for m in range(8):
                fsl = 2 + m % 2
                use_bank(fsl)
                for kb in range(8):
                    sdmm = P.em("tensor", lambda e, m=m, kb=kb, sdv=sdv, fsl=fsl:
                                e.matmul(ps[fsl][:, :],
                                         sdv[:, kb, m * 128:(m + 1) * 128],
                                         shx[:, kb, :], start=(kb == 0),
                                         stop=(kb == 7)),
                                inc=(PE if kb == 7 else None))
                P.wt("scalar", PE, sdmm)
                esl = m % 2
                P.wt("scalar", X2, etmp_cons[esl])
                ev = P.em("scalar", lambda e, m=m, fsl=fsl, esl=esl: e.activation(
                    out=etmp[:, esl, :], in_=ps[fsl][:, :], func=AF.Copy), inc=ACT)
                set_bank(fsl, ACT, ev)
                P.wt("gpsimd", ACT, ev)
                ad = P.em("gpsimd", lambda e, m=m, esl=esl: e.tensor_tensor(
                    out=ffn_acc[:, m, :], in0=ffn_acc[:, m, :],
                    in1=etmp[:, esl, :], op=ALU.add), inc=X2)
                etmp_cons[esl] = ad
                fst["ffn_add"] = ad
            st["sdn_pe"] = sdmm
            wdone(fst["sdnh"], sdmm)
            if dump and kk == dumpc:
                P.wt("sync", X2, fst["ffn_add"])
                P.em("sync", lambda e: e.dma_start(
                    out=dumps["d_ffn"][:, :], in_=ffn_acc[:, :, :]), inc=ST, k=16)
                dump_barrier()

        def seg_resid():
            ad = 0
            for m in range(8):
                ad = P.em("gpsimd", lambda e, m=m: e.tensor_tensor(
                    out=ffn_acc[:, m, :], in0=ffn_acc[:, m, :], in1=xf[:, m, :],
                    op=ALU.add), inc=X2)
            st["resid_x2"] = ad
            fst["ffn_add"] = ad

        def seg_ln3(g):
            for m in range(8):
                sl = m % 2
                P.wt("tensor", X2, fst["ffn_add"])
                use_bank(6 + sl)
                tp = P.em("tensor", lambda e, g=g, m=m, sl=sl: e.transpose(
                    psT[sl][:, :], ffn_acc[:, m, g * 128:(g + 1) * 128],
                    ident_bf[:]), inc=PE)
                st["ln3_tp"] = tp
                P.wt("scalar", PE, tp)
                if m < 2:
                    P.wt("scalar", X2, th_cons[0])
                    P.wt("scalar", X2, th_cons[1])
                    P.wt("scalar", DVE, st["x3_stats_dve"])
                ev = P.em("scalar", lambda e, m=m, sl=sl: e.activation(
                    out=x3_tm[:, m * 128:(m + 1) * 128], in_=psT[sl][:, :],
                    func=AF.Copy), inc=ACT)
                set_bank(6 + sl, ACT, ev)
            x3_ev = ev

            def ln3_out(sc_col, nb_col, g=g):
                P.wt("scalar", ST, st["out_ev"])
                P.wt("scalar", ACT, x3_ev)
                a3 = P.em("scalar", lambda e, sc_col=sc_col, nb_col=nb_col:
                          e.activation(out=out_tm[:, :], in_=x3_tm[:, :],
                                       func=AF.Identity,
                                       scale=sml[:, sc_col:sc_col + 1],
                                       bias=sml[:, nb_col:nb_col + 1]), inc=ACT)
                return a3
            P.wt("vector", ACT, x3_ev)
            a3, v2ms = ln_core(lambda: x3_tm[:, :], 64, ln3_out)
            st["x3_stats_dve"] = v2ms
            th_cons[0] = 0
            th_cons[1] = 0
            P.wt("sync", ACT, a3)
            stv = P.em("sync", lambda e, g=g: e.dma_start(
                out=out_d[kk * TC + g * 128: kk * TC + (g + 1) * 128, :],
                in_=out_tm[:, :]), inc=ST, k=16)
            st["out_ev"] = stv

        for g in range(nti):
            segs.append(lambda g=g: seg_router(g))
        segs.append(seg_comb)
        for e_ in range(NE):
            segs.append(lambda e_=e_: seg_w1(e_, 0))
            segs.append(lambda e_=e_: seg_w1(e_, 1))
            segs.append(lambda e_=e_: seg_w2(e_, 0))
            segs.append(lambda e_=e_: seg_w2(e_, 1))
        segs.append(seg_sgate)
        for mg in range(8):
            segs.append(lambda mg=mg: seg_gu(mg))
        segs.append(seg_sdn)
        segs.append(seg_resid)
        for g in range(nti):
            segs.append(lambda g=g: seg_ln3(g))
        return segs

    if sched is None:
        base = [9 + (9 * e) // 2 for e in range(8)]
        sched = ([1, 2, 3, 4, 5] +
                 sum([[b, b + 1, b + 2, b + 3] for b in base], []) +
                 [46] + [47 + mg for mg in range(8)] +
                 [56, 58] + [59, 60, 61, 62])
    try:
        segs = []
        nseg = [0]
        for k in range(nch + 1):
            if k > 0:
                kk = k - 1
                wo0 = wreq(wout_src(0), "out")
                wo1 = wreq(wout_src(1), "out")
                P.wt("tensor", DVE, st["ygfin"])
                womm = None
                for m in range(8):
                    sl = m % 2
                    use_bank(sl)
                    wuse(wo0)
                    wuse(wo1)
                    wv0 = wview(wo0, D)
                    wv1 = wview(wo1, D)
                    for kb in range(16):
                        wv = wv0 if kb < 8 else wv1
                        womm = P.em("tensor", lambda e, m=m, kb=kb, wv=wv, sl=sl:
                                    e.matmul(ps[sl][:, :],
                                             wv[:, kb % 8, m * 128:(m + 1) * 128],
                                             yg_bf[:, kb, :], start=(kb == 0),
                                             stop=(kb == 15)),
                                    inc=(PE if kb == 15 else None))
                    P.wt("scalar", PE, womm)
                    if m < 2:
                        P.wt("scalar", PE, st["sdn_pe"])
                        P.wt("scalar", X2, st["gu_x2"])
                    ev = P.em("scalar", lambda e, m=m, sl=sl: e.activation(
                        out=mo_fm[:, m, :], in_=ps[sl][:, :], func=AF.Copy),
                        inc=ACT)
                    set_bank(sl, ACT, ev)
                    st["mo_ev"][m] = ev
                wdone(wo0, womm)
                wdone(wo1, womm)
                st["wo_pe"] = womm
                if dump and kk == dumpc:
                    P.wt("sync", ACT, st["mo_ev"][7])
                    P.em("sync", lambda e: e.dma_start(
                        out=dumps["d_mo"][:, :], in_=mo_fm[:, :, :]), inc=ST, k=16)
                    dump_barrier()
                ckpt("wout")
                x2v = 0
                for i in range(nti):
                    for m in range(8):
                        sl = m % 2
                        P.wt("tensor", ACT, st["mo_ev"][m])
                        use_bank(6 + sl)
                        tp = P.em("tensor", lambda e, i=i, m=m, sl=sl: e.transpose(
                            psT[sl][:, :], mo_fm[:, m, i * 128:(i + 1) * 128],
                            ident_bf[:]), inc=PE)
                        P.wt("vector", PE, tp)
                        x2v = P.em("vector", lambda e, i=i, m=m, sl=sl:
                                   e.scalar_tensor_tensor(
                            out=x_tm[:, i, m * 128:(m + 1) * 128],
                            in0=x_tm[:, i, m * 128:(m + 1) * 128], scalar=2.0,
                            op0=ALU.mult, in1=psT[sl][:, :], op1=ALU.add), inc=DVE)
                        set_bank(6 + sl, DVE, x2v)
                st["x2_dve"] = x2v
                ckpt("x2")
                xf_act = 0
                P.wt("vector", DVE, st["x2_dve"])
                P.wt("scalar", DVE, st["x2_dve"])
                ln2_a2, ln2_v4 = ln_group(
                    [(lambda i=i: x_tm[:, i, :]) for i in range(nti)],
                    [32 + 8 * i for i in range(nti)])
                for i in range(nti):
                    isl = i % 2
                    sc_col, nb_col = 32 + 8 * i + 6, 32 + 8 * i + 7
                    P.wt("scalar", DVE, ln2_v4[i])
                    P.wt("scalar", ACT, ln2_a2[i])
                    P.wt("scalar", PE, xslot_pe[isl])
                    a3 = P.em("scalar", lambda e, i=i, isl=isl, sc_col=sc_col,
                              nb_col=nb_col: e.activation(
                        out=x2slot[isl], in_=x_tm[:, i, :], func=AF.Identity,
                        scale=sml[:, sc_col:sc_col + 1],
                        bias=sml[:, nb_col:nb_col + 1]), inc=ACT)
                    st["ln2_act"] = a3
                    for kb in range(8):
                        sl = kb % 2
                        P.wt("tensor", ACT, a3)
                        use_bank(6 + sl)
                        tp = P.em("tensor", lambda e, i=i, kb=kb, sl=sl, isl=isl:
                                  e.transpose(psT[sl][:, :],
                                              x2slot[isl][:, kb * 128:(kb + 1) * 128],
                                              ident_bf[:]), inc=PE)
                        xslot_pe[isl] = tp
                        st["xf_tp"] = tp
                        P.wt("scalar", PE, tp)
                        P.wt("scalar", X2, st["resid_x2"])
                        xf_act = P.em("scalar", lambda e, i=i, kb=kb, sl=sl:
                                      e.activation(
                            out=xf[:, kb, i * 128:(i + 1) * 128],
                            in_=psT[sl][:, :], func=AF.Copy), inc=ACT)
                        set_bank(6 + sl, ACT, xf_act)
                st["xf_act"] = xf_act
                if dump and kk == dumpc:
                    P.wt("sync", ACT, xf_act)
                    P.em("sync", lambda e: e.dma_start(
                        out=dumps["d_xf"][:, :], in_=xf[:, :, :]), inc=ST, k=16)
                    dump_barrier()
                segs = ffn_segments(kk)
                nseg[0] = 0
                ckpt("ln2")

            def pump(upto_it):
                while nseg[0] < len(segs) and sched[nseg[0]] <= upto_it:
                    segs[nseg[0]]()
                    nseg[0] += 1

            def drain():
                while nseg[0] < len(segs):
                    segs[nseg[0]]()
                    nseg[0] += 1

            if k == nch:
                drain()
                break

            c = k
            if c == 0:
                P.wt("gpsimd", ACT, p0_act)
                P.wt("gpsimd", DVE, p0_dve)
            else:
                P.wt("gpsimd", ACT, st["ln2_act"])
            xv = P.em("gpsimd", lambda e, c=c: e.dma_start(
                out=x_tm[:, :, :],
                in_=x_d[c * TC:(c + 1) * TC, :].rearrange("(i p) d -> p i d", p=128)),
                inc=X, k=16)

            xn_ev_last = 0
            h_win0 = wreq(win_src(0), "in")
            h_win1 = wreq(win_src(1), "in")
            P.wt("vector", X, xv)
            if c == 0:
                P.wt("vector", ACT, p0_act)
                P.wt("vector", DVE, p0_dve)
            P.wt("scalar", X, xv)
            if c == 0:
                P.wt("scalar", ACT, p0_act)
            ln1_a2, ln1_v4 = ln_group(
                [(lambda i=i: x_tm[:, i, :]) for i in range(nti)],
                [8 * i for i in range(nti)])
            for i in range(nti):
                sc_col, nb_col = 8 * i + 6, 8 * i + 7
                P.wt("scalar", DVE, ln1_v4[i])
                P.wt("scalar", ACT, ln1_a2[i])
                P.wt("scalar", PE, st["xf_tp"])
                a3 = P.em("scalar", lambda e, i=i, sc_col=sc_col, nb_col=nb_col:
                          e.activation(out=xt_bf[:, 0:D], in_=x_tm[:, i, :],
                                       func=AF.Identity,
                                       scale=sml[:, sc_col:sc_col + 1],
                                       bias=sml[:, nb_col:nb_col + 1]), inc=ACT)
                P.wt("vector", ACT, a3)
                vm = P.em("vector", lambda e: e.tensor_tensor(
                    out=xt_bf[:, D:2 * D], in0=xt_bf[:, 0:D], in1=nw_rep[:],
                    op=ALU.mult), inc=DVE)
                P.wt("vector", DVE, vm)
                P.wt("vector", PE, st["xn_tp"])
                v4 = P.em("vector", lambda e: e.tensor_tensor(
                    out=xn_tm[:], in0=xt_bf[:, D:2 * D], in1=nb_rep[:],
                    op=ALU.add), inc=DVE)
                st["xn_dve"] = v4
                P.wt("tensor", DVE, v4)
                if c == 0 and i == 0:
                    P.wt("tensor", LD, consts_ld)
                for kb in range(8):
                    sl = kb % 2
                    use_bank(6 + sl)
                    tp = P.em("tensor", lambda e, kb=kb, sl=sl: e.transpose(
                        psT[sl][:, :], xn_tm[:, kb * 128:(kb + 1) * 128],
                        ident_bf[:]), inc=PE)
                    st["xn_tp"] = tp
                    P.wt("scalar", PE, tp)
                    if i == 0 and kb == 0:
                        P.wt("scalar", SCAN, 128 * c)
                    ev = P.em("scalar", lambda e, kb=kb, i=i, sl=sl: e.activation(
                        out=xn_fm[:, kb, i * 128:(i + 1) * 128], in_=psT[sl][:, :],
                        func=AF.Copy), inc=ACT)
                    set_bank(6 + sl, ACT, ev)
                    xn_ev_last = ev
            if dump and c == dumpc:
                P.wt("sync", ACT, xn_ev_last)
                P.em("sync", lambda e: e.dma_start(
                    out=dumps["d_xn"][:, :], in_=xn_fm[:, :, :]), inc=ST, k=16)
                dump_barrier()
            ckpt("ln1")

            h_g0 = wreq(gate_src(0), "gate")
            P.wt("tensor", ACT, xn_ev_last)
            wuse(h_win0)
            wuse(h_win1)
            wiv0 = wview(h_win0, DI)
            wiv1 = wview(h_win1, DI)
            zmm = None
            for m in range(16):
                sl = m % 2
                use_bank(sl)
                for kb in range(8):
                    wv = wiv0 if kb < 4 else wiv1
                    zmm = P.em("tensor", lambda e, m=m, kb=kb, wv=wv, sl=sl:
                               e.matmul(ps[sl][:, :],
                                        wv[:, kb % 4, m * 128:(m + 1) * 128],
                                        xn_fm[:, kb, :], start=(kb == 0),
                                        stop=(kb == 7)),
                               inc=(PE if kb == 7 else None))
                P.wt("scalar", PE, zmm)
                if m == 0:
                    P.wt("scalar", DVE, st["u_last"])
                ev = P.em("scalar", lambda e, m=m, sl=sl: e.activation(
                    out=z_bf[:, m, :], in_=ps[sl][:, :], func=AF.Copy), inc=ACT)
                set_bank(sl, ACT, ev)
            wdone(h_win0, zmm)
            wdone(h_win1, zmm)
            h_g1 = wreq(gate_src(1), "gate")
            ckpt("z")

            wuse(h_g0)
            wuse(h_g1)
            gv0 = wview(h_g0, DI)
            gv1 = wview(h_g1, DI)
            gmm = None
            for m in range(16):
                sl = m % 2
                use_bank(sl)
                for kb in range(8):
                    wv = gv0 if kb < 4 else gv1
                    gmm = P.em("tensor", lambda e, m=m, kb=kb, wv=wv, sl=sl:
                               e.matmul(ps[sl][:, :],
                                        wv[:, kb % 4, m * 128:(m + 1) * 128],
                                        xn_fm[:, kb, :], start=(kb == 0),
                                        stop=(kb == 7)),
                               inc=(PE if kb == 7 else None))
                P.wt("scalar", PE, gmm)
                if m == 0:
                    P.wt("scalar", DVE, st["ygfin"])
                ev = P.em("scalar", lambda e, m=m, sl=sl: e.activation(
                    out=gate_bf[:, m, :], in_=ps[sl][:, :], func=AF.Silu), inc=ACT)
                set_bank(sl, ACT, ev)
                st["gate_ev"] = ev
            wdone(h_g0, gmm)
            wdone(h_g1, gmm)
            st["gate_pe"] = gmm
            P.wt("vector", PE, gmm)
            z0 = P.em("vector", lambda e: e.memset(dA4[:, :, :, 0:2], 0.0), inc=DVE)
            if c == 0:
                u4z_ms = P.em("vector", lambda e: e.memset(U4[:, :, 0:1], 0.0),
                              inc=DVE)
                P.wt("vector", DVE, u4z_ms)
            h_dt = [wreq(dt_src(0), "dt"), wreq(dt_src(1), "dt")]
            ckpt("gate")

            P.wt("tensor", STG, max(wb_ld, wc_ld))
            P.wt("tensor", ACT, st["bc_ev"])
            bc_last = None
            for (pbc, wsb, dstt) in [(ps[2][0:16, :], wbb, btb),
                                     (ps[3][0:16, :], wcb, ctb)]:
                use_bank(2 if dstt is btb else 3)
                last_mm = None
                for kb in range(16):
                    last_mm = P.em("tensor", lambda e, kb=kb, pbc=pbc, wsb=wsb:
                                   e.matmul(pbc[:, :], wsb[:, kb * DS:(kb + 1) * DS],
                                            z_bf[:, kb, :], start=(kb == 0),
                                            stop=(kb == 15)),
                                   inc=(PE if kb == 15 else None))
                bc_last = last_mm
                P.wt("scalar", PE, last_mm)
                P.wt("scalar", PE, st["rep_pe"])
                bc_ev = P.em("scalar", lambda e, pbc=pbc, dstt=dstt: e.activation(
                    out=dstt[:, :], in_=pbc[:, :], func=AF.Copy), inc=ACT)
                set_bank(2 if dstt is btb else 3, ACT, bc_ev)
            st["bc_ev"] = bc_ev

            def rep_half(sgrp, srct, dstr, xwar=0):
                last = 0
                P.wt("tensor", ACT, st["bc_ev"])
                if xwar:
                    P.wt("tensor", DVE, xwar)
                mm = None
                for j in range(4):
                    s_ = sgrp * 4 + j
                    sl = (s_ + (0 if srct is btb else 4)) % 2
                    use_bank(sl)
                    mm = P.em("tensor", lambda e, s_=s_, srct=srct, sl=sl:
                              e.matmul(ps[sl][:, :],
                                       oh16[:, s_ * 128:(s_ + 1) * 128],
                                       srct[:, :], start=True, stop=True),
                              inc=PE)
                    P.wt("scalar", PE, mm)
                    ev = P.em("scalar", lambda e, j=j, dstr=dstr, sl=sl:
                              e.activation(out=dstr[:, j, :],
                                           in_=ps[sl][:, :], func=AF.Copy),
                              inc=ACT)
                    set_bank(sl, ACT, ev)
                    last = ev
                st["rep_pe"] = mm
                return last

            brep_bufs = [brepA, brepB]
            rep_last = rep_half(0, btb, brep_bufs[0], xwar=st["u_last"])
            P.wt("tensor", POOL, 128 * c)
            crep_last = rep_half(0, ctb, crep4)
            rep_last = max(rep_last, crep_last)
            brep_ready = [rep_last, 0, 0, 0]

            dt_clip = []
            defer_clips = []
            dt_prev_act = 0
            for pss in range(2):
                for i_ in range(8):
                    use_bank(i_)
                if pss == 0:
                    P.wt("scalar", DVE, st["xn_dve"])
                dtmm = None
                h_q = h_dt if pss == 0 else [wreq(dt_src(0), "dt"),
                                             wreq(dt_src(1), "dt")]

                for q in range(4):
                    if q >= 2:
                        h_q.append(wreq(dt_src(q), "dt"))
                    hq = h_q[q]
                    wuse(hq)
                    qv = wview(hq, DI)
                    last_q = None
                    for kbq in range(4):
                        kb = q * 4 + kbq
                        for m in range(8):
                            last_q = P.em("tensor", lambda e, kb=kb, kbq=kbq, m=m,
                                          qv=qv, pss=pss: e.matmul(
                                ps[m][:, :],
                                qv[:, kbq, (pss * 8 + m) * 128:(pss * 8 + m + 1) * 128],
                                z_bf[:, kb, :], start=(kb == 0), stop=(kb == 15)),
                                inc=(PE if (kbq == 3 and m == 7) else None))
                    wdone(hq, last_q)
                    dtmm = last_q
                P.wt("scalar", PE, dtmm)
                for m in range(8):
                    mm_ = pss * 8 + m
                    P.wt("scalar", DVE, dt_clip[-2] if len(dt_clip) >= 2 else 0)
                    a1 = P.em("scalar", lambda e, m=m, mm_=mm_: e.activation(
                        out=tmp_e[:, :], in_=ps[m][:, :], func=AF.Exp,
                        bias=bdt_sb[:, mm_:mm_ + 1]), inc=ACT)
                    set_bank(m, ACT, a1)
                    P.wt("scalar", ACT, a1)
                    a2 = P.em("scalar", lambda e, m=m: e.activation(
                        out=tmp_l3[:, m % 2, :], in_=tmp_e[:, :], func=AF.Ln,
                        bias=1.0), inc=ACT)
                    if pss == 0:
                        P.wt("vector", ACT, a2)
                        v = P.em("vector", lambda e, m=m, mm_=mm_: e.tensor_scalar(
                            out=dt_bf[:, mm_, :], in0=tmp_l3[:, m % 2, :],
                            scalar1=1.0, scalar2=0.001, op0=ALU.min,
                            op1=ALU.max), inc=DVE)
                        dt_clip.append(v)
                    else:
                        defer_clips.append((m, mm_, a2))
                if pss == 1:
                    P.wt("vector", PE, dtmm)
                    dtz_ms = P.em("vector", lambda e: e.tensor_tensor(
                        out=z_bf[:, 0:8, :], in0=dt_bf[:, 0:8, :],
                        in1=z_bf[:, 0:8, :], op=ALU.mult), inc=DVE)
                    for (m, mm_, a2) in defer_clips:
                        P.wt("vector", ACT, a2)
                        v = P.em("vector", lambda e, m=m, mm_=mm_: e.tensor_scalar(
                            out=dt_bf[:, mm_, :], in0=tmp_l3[:, m % 2, :],
                            scalar1=1.0, scalar2=0.001, op0=ALU.min,
                            op1=ALU.max), inc=DVE)
                        dt_clip.append(v)
            st["dt_clip15"] = dt_clip[15]
            ckpt("dt")

            dtz_hi_emitted = [False]

            def emit_dtz_hi():
                if dtz_hi_emitted[0]:
                    return
                dtz_hi_emitted[0] = True
                P.em("vector", lambda e: e.tensor_tensor(
                    out=z_bf[:, 8:16, :], in0=dt_bf[:, 8:16, :],
                    in1=z_bf[:, 8:16, :], op=ALU.mult), inc=DVE)
            if dump:
                emit_dtz_hi()
            ckpt("bc_dtz")
            if dump and c == dumpc:
                P.wt("sync", DVE, DVE.n)
                P.wt("sync", ACT, st["gate_ev"])
                P.em("sync", lambda e: e.dma_start(out=dumps["d_z"][:, :], in_=z_bf[:, :, :]), inc=ST, k=16)
                P.em("sync", lambda e: e.dma_start(out=dumps["d_gate"][:, :], in_=gate_bf[:, :, :]), inc=ST, k=16)
                P.em("sync", lambda e: e.dma_start(out=dumps["d_dt"][:, :], in_=dt_bf[:, :, :]), inc=ST, k=16)
                P.wt("sync", ACT, st["bc_ev"])
                P.em("sync", lambda e: e.dma_start(out=dumps["d_Bt"][:, :], in_=btb[:, :]), inc=ST, k=16)
                P.em("sync", lambda e: e.dma_start(out=dumps["d_Ct"][:, :], in_=ctb[:, :]), inc=ST, k=16)
                dump_barrier()

            scan_base = SCAN.n
            if c == 0:
                P.wt("vector", DVE, hcol_ms)
                P.wt("scalar", DVE, aneg_ms)
            scan_ms = {}
            red_ms = {}
            u_ms_i = {}
            da_ms_i = {}
            last_yacc = 0
            pend_hv = None
            for sgrp in range(4):
                if sgrp > 0:
                    P.wt("tensor", POOL,
                         pool_ms.get(64 * c + (sgrp - 1) * 16 + 15, 0))
                    crep_last = rep_half(sgrp, ctb, crep4)
                for db in range(16):
                    it = sgrp * 16 + db
                    if it == 3:
                        emit_dtz_hi()
                    pump(it)
                    if pend_hv is not None:
                        hv = P.em("vector", lambda e, a=pend_hv: e.tensor_copy(
                            out=hcol[:, a[1]:a[1] + 4], in_=H4[:, a[0], :, TC + 1]),
                            inc=DVE)
                        pend_hv = None
                    P.wt("scalar", DVE, z0)
                    P.wt("scalar", DVE, dt_clip[db])
                    da_h = [0, 0]
                    for h in range(2):
                        P.wt("scalar", SCAN, scan_base + 2 * (it - 1) + h + 1)
                        for j in range(2):
                            s_ = sgrp * 4 + 2 * h + j
                            v = P.em("scalar", lambda e, db=db, s_=s_, h=h, j=j:
                                     e.activation(
                                out=dA4[:, h, j, 2:TC + 2], in_=dt_bf[:, db, :],
                                func=AF.Exp,
                                scale=aneg[:, db * DS + s_: db * DS + s_ + 1]),
                                inc=ACT)
                        da_h[h] = v
                    if db == 0 and sgrp == 0:
                        P.wt("vector", ACT, rep_last)
                        P.wt("vector", DVE, dtz_ms)
                    P.wt("vector", ACT, rep_last)
                    cidx = db * 16 + sgrp * 4
                    P.em("vector", lambda e, cidx=cidx: e.tensor_copy(
                        out=U4[:, :, 1], in_=hcol[:, cidx:cidx + 4]))
                    pit = it - 1
                    pslot = pit % 2
                    psg, pdb = pit // 16, pit % 16
                    slot = it % 2
                    sc = None
                    sc_h = [None, None]
                    rv = 0
                    if db == 8 and sgrp < 3:
                        _xw = (st["u_last"] if sgrp == 0
                               else u_ms_i[(sgrp - 1, 15)])
                        brep_ready[sgrp + 1] = rep_half(
                            sgrp + 1, btb, brep_bufs[(sgrp + 1) % 2], xwar=_xw)
                    if db == 0:
                        P.wt("vector", ACT, brep_ready[sgrp])
                    bcur = brep_bufs[sgrp % 2]
                    uv = None
                    for j4 in (0, 2, 1, 3):
                        uv = P.em("vector", lambda e, db=db, j4=j4, bcur=bcur:
                                  e.tensor_tensor(
                            out=U4[:, j4, 2:TC + 2], in0=z_bf[:, db, :],
                            in1=bcur[:, j4, :], op=ALU.mult), inc=DVE)
                    u_ms_i[(sgrp, db)] = uv
                    for h in range(2):
                        P.wt("vector", DVE, uv)
                        P.wt("vector", ACT, da_h[h])
                        if it >= 2:
                            P.wt("vector", POOL, pool_ms.get(64 * c + it - 2, 0))
                        sc = P.em("vector", lambda e, slot=slot, h=h:
                                  e.tensor_tensor_scan(
                            out=H4[:, slot, 2 * h:2 * h + 2, :].rearrange(
                                "p s t -> p (s t)"),
                            data0=dA4[:, h, :, :].rearrange("p s t -> p (s t)"),
                            data1=U4[:, 2 * h:2 * h + 2, :].rearrange(
                                "p s t -> p (s t)"),
                            initial=0.0, op0=ALU.mult, op1=ALU.add), inc=SCAN)
                        sc_h[h] = sc
                        if h == 0 and it >= 1:
                            if pit == 0:
                                P.wt("vector", PE, st["wo_pe"])
                            P.wt("vector", POOL, pool_ms[64 * c + pit])
                            rv = P.em("vector", lambda e: e.tensor_tensor(
                                out=t1r[:, :, :], in0=t1r[:, :, :],
                                in1=t2r[:, :, :], op=ALU.add), inc=RED)
                            red_ms[pit] = rv
                            if psg == 0:
                                last_yacc = P.em("vector", lambda e, pdb=pdb:
                                                 e.tensor_copy(
                                    out=yg_bf[:, pdb, :], in_=t1r[:, 0, :]),
                                    inc=DVE)
                            else:
                                last_yacc = P.em("vector", lambda e, pdb=pdb:
                                                 e.tensor_tensor(
                                    out=yg_bf[:, pdb, :], in0=yg_bf[:, pdb, :],
                                    in1=t1r[:, 0, :], op=ALU.add), inc=DVE)
                            last_yacc = P.em("vector", lambda e, pdb=pdb:
                                             e.tensor_tensor(
                                out=yg_bf[:, pdb, :], in0=yg_bf[:, pdb, :],
                                in1=t1r[:, 1, :], op=ALU.add), inc=DVE)
                    scan_ms[it] = sc
                    pend_hv = (slot, cidx)
                    P.wt("gpsimd", SCAN, sc_h[0])
                    if it >= 1:
                        P.wt("gpsimd", RED, red_ms[pit])
                        P.wt("gpsimd", DVE, last_yacc)
                    else:
                        P.wt("gpsimd", DVE, st["ygfin"])
                    if db == 0:
                        P.wt("gpsimd", ACT, crep_last)
                    P.em("gpsimd", lambda e, slot=slot: e.tensor_tensor(
                        out=t1r[:, :, :], in0=H4[:, slot, 0:2, 2:TC + 2],
                        in1=crep4[:, 0:2, :], op=ALU.mult), inc=POOL)
                    P.wt("gpsimd", SCAN, sc_h[1])
                    pm_ = P.em("gpsimd", lambda e, slot=slot: e.tensor_tensor(
                        out=t2r[:, :, :], in0=H4[:, slot, 2:4, 2:TC + 2],
                        in1=crep4[:, 2:4, :], op=ALU.mult), inc=POOL)
                    pool_ms[64 * c + it] = pm_
            st["u_last"] = u_ms_i[(3, 15)]
            if pend_hv is not None:
                hv = P.em("vector", lambda e, a=pend_hv: e.tensor_copy(
                    out=hcol[:, a[1]:a[1] + 4], in_=H4[:, a[0], :, TC + 1]),
                    inc=DVE)
                pend_hv = None
            it = 63
            P.wt("vector", POOL, pool_ms[64 * c + it])
            rv = P.em("vector", lambda e: e.tensor_tensor(
                out=t1r[:, :, :], in0=t1r[:, :, :], in1=t2r[:, :, :],
                op=ALU.add), inc=RED)
            red_ms[it] = rv
            P.wt("vector", RED, rv)
            last_yacc = P.em("vector", lambda e: e.tensor_tensor(
                out=yg_bf[:, 15, :], in0=yg_bf[:, 15, :], in1=t1r[:, 0, :],
                op=ALU.add), inc=DVE)
            last_yacc = P.em("vector", lambda e: e.tensor_tensor(
                out=yg_bf[:, 15, :], in0=yg_bf[:, 15, :], in1=t1r[:, 1, :],
                op=ALU.add), inc=DVE)
            P.wt("vector", DVE, last_yacc)
            P.wt("vector", ACT, st["gate_ev"])
            ygf = 0
            for db in range(16):
                ygf = P.em("vector", lambda e, db=db: e.tensor_tensor(
                    out=yg_bf[:, db, :], in0=yg_bf[:, db, :], in1=gate_bf[:, db, :],
                    op=ALU.mult), inc=YG)
            st["ygfin"] = P.em("vector", lambda e: e.tensor_copy(
                out=sml[:, 87:88], in_=sml[:, 87:88]), inc=DVE)
            if dump and c == dumpc:
                P.wt("sync", YG, 16 * (c + 1))
                P.em("sync", lambda e: e.dma_start(out=dumps["d_yg"][:, :], in_=yg_bf[:, :, :]), inc=ST, k=16)
                dump_barrier()
            ckpt("scan")
            drain()
            ckpt("drain")

    except _Trunc:
        pass

    es1p.close()
    es1.close()

    P.wt("sync", ST, ST.n)
    with nc.Block() as blk:
        for eng, meth in [("sync", blk.sync), ("scalar", blk.scalar),
                          ("vector", blk.vector), ("tensor", blk.tensor),
                          ("gpsimd", blk.gpsimd)]:
            def runner(e, eng=eng):
                for thn in P.p[eng]:
                    thn(e)
            meth(runner)
    es.close()
    return nc, dumps


def host_inputs(inp, b):
    g = lambda k: np.ascontiguousarray(inp[k])
    return {
        "norm_w": g("L0_attn_norm_w").reshape(1, D),
        "norm_b": g("L0_attn_norm_b").reshape(1, D),
        "W_in": g("L0_attn_W_in"),
        "W_gate": g("L0_attn_W_gate"),
        "W_dt": g("L0_attn_W_dt"),
        "b_dt": g("L0_attn_b_dt").reshape(DI // 128, 128),
        "W_B": g("L0_attn_W_B"),
        "W_C": g("L0_attn_W_C"),
        "A_log": g("L0_attn_A_log"),
        "W_out": g("L0_attn_W_out"),
        "router_w": g("L0_ffn_router_w"),
        "w1": g("L0_ffn_w1").reshape(NE * D, D),
        "w2": g("L0_ffn_w2").reshape(NE * D, D),
        "sgu": g("L0_ffn_shared_gate_up"),
        "sdown": g("L0_ffn_shared_down"),
        "sgw": g("L0_ffn_shared_gate_w").reshape(D // 128, 128),
    }


def kernel(**inputs):
    from concourse.bass_utils import run_bass_kernel_spmd
    X = np.asarray(inputs["L0_attn_x"])
    nc, _ = build(T=4096, TC=512, dump=False)
    base = host_inputs(inputs, 0)
    in_maps = []
    for b in range(8):
        m = dict(base)
        m["x"] = np.ascontiguousarray(X[b])
        in_maps.append(m)
    res = run_bass_kernel_spmd(nc, in_maps, core_ids=list(range(8)))
    return np.stack([r["out"] for r in res.results], axis=0)
